# version 01
# baseline (speedup 1.0000x reference)
import contextlib
import numpy as np
import ml_dtypes
import concourse.bass as bass
import concourse.mybir as mybir
from concourse.bass_utils import run_bass_kernel_spmd
from concourse.aot_env import allow_ici_all_to_all

F32 = mybir.dt.float32
BF16 = mybir.dt.bfloat16
AF = mybir.ActivationFunctionType
ALU = mybir.AluOpType
AX = mybir.AxisListType

NCORE = 8
TOK = 1024
S_ALL = 8192
DM = 2048
KD = 16
HID = 5632
HT = 44
QR = 1536
QKD = 12
KVR = 512
KVKD = 4
SCALE = float(1.0 / np.sqrt(192.0))
WCNT = float(HID * DM)
ROWW = 256 * HID
ROWL = (QR + KVR + 128) * TOK
QOFF = QR * TOK
KVOFF = (QR + KVR) * TOK
A2AB = 2 * 128 * TOK

PHASE_LIMIT = 99
DBG = False


class Builder:
    ENGS = ("t", "s", "v", "g", "y")

    def __init__(self):
        self.recs = {e: [] for e in self.ENGS}
        self.cnt = {}
        self.sems = {}
        self.last_inc = {e: None for e in self.ENGS}
        self.since_barrier = {e: 0 for e in self.ENGS}

    def sem(self, name):
        self.cnt.setdefault(name, 0)
        return name

    def emit(self, e, fn, inc=None, inc_n=1):
        rec = {"fn": fn, "incs": []}
        if inc is not None:
            self.sem(inc)
            self.cnt[inc] += inc_n
            rec["incs"].append((inc, inc_n))
            self.last_inc[e] = (inc, self.cnt[inc])
        self.recs[e].append(rec)
        self.since_barrier[e] += 1
        return rec

    def dma(self, e, out, in_, inc, **kw):
        assert e in ("y", "g")
        return self.emit(e, lambda eng: eng.dma_start(out=out, in_=in_, **kw),
                         inc=inc, inc_n=16)

    def wait(self, e, sname, cnt=None):
        if cnt is None:
            cnt = self.cnt.get(sname, 0)
        if cnt <= 0:
            return
        self.sem(sname)
        self.recs[e].append(
            {"fn": (lambda s_, c_: lambda eng: eng.wait_ge(self.sems[s_], c_))(sname, cnt),
             "incs": []})
        self.since_barrier[e] += 1

    def mark(self, sname):
        return self.cnt.get(sname, 0)

    def barrier(self):
        marks = {}
        for e in self.ENGS:
            if self.since_barrier[e] == 0:
                continue
            rec = self.recs[e][-1]
            if not rec["incs"]:
                pname = self.sem("prog_" + e)
                self.cnt[pname] += 1
                rec["incs"].append((pname, 1))
                self.last_inc[e] = (pname, self.cnt[pname])
            marks[e] = self.last_inc[e]
        for e in self.ENGS:
            for o, (sname, val) in marks.items():
                if o == e:
                    continue
                self.wait(e, sname, val)
        for e in self.ENGS:
            self.since_barrier[e] = 0

    def replay(self, block):
        b = self

        def make(engname):
            def body(eng):
                for rec in b.recs[engname]:
                    inst = rec["fn"](eng)
                    for (sname, n) in rec["incs"]:
                        inst.then_inc(b.sems[sname], n)
            return body

        block.tensor(make("t"))
        block.scalar(make("s"))
        block.vector(make("v"))
        block.gpsimd(make("g"))
        block.sync(make("y"))


def build_program():
    nc = bass.Bass("TRN2", target_bir_lowering=False)
    B = Builder()
    E, W, D, M = B.emit, B.wait, B.dma, B.mark

    def ein(name, shape, dt=F32):
        return nc.dram_tensor(name, shape, dt, kind="ExternalInput").ap()

    xT = ein("xT", [DM, TOK])
    w1_s = ein("w1_s", [256, HID]); w2_s = ein("w2_s", [256, HID]); w3_s = ein("w3_s", [HID, 256])
    wq_down = ein("wq_down", [DM, QR]); wkv_down = ein("wkv_down", [DM, KVR])
    wk_rope = ein("wk_rope", [DM, 64]); wo_in = ein("wo", [DM, DM])
    wq_head = ein("wq_head", [QR, 384])
    wkv_k = ein("wkv_k", [KVR, 256]); wkv_v = ein("wkv_v", [KVR, 256])
    qnw_t = ein("qnw_t", [128, QKD]); kvnw_t = ein("kvnw_t", [128, KVKD])
    onesr_in = ein("onesr", [1, 128])
    idf_in = ein("idf", [128, 128]); idb_in = ein("idb", [128, 128], BF16)
    pmT_in = ein("pmT", [128, 128], BF16)
    maskp_in = ein("maskp", [128, 4 * 512], BF16)
    masktp_in = ein("masktp", [128, 4 * 512], BF16)
    ropecq = ein("ropecq", [128, S_ALL], BF16); ropesq = ein("ropesq", [128, S_ALL], BF16)
    ropeck = ein("ropeck", [128, TOK], BF16); ropesk = ein("ropesk", [128, TOK], BF16)

    yout = nc.dram_tensor("y", [DM, TOK], F32, kind="ExternalOutput").ap()

    g_in = nc.dram_tensor("g_in", [1, 4], F32).ap()
    g_out = nc.dram_tensor("g_out", [1, 4], F32).ap()
    agw_in = [nc.dram_tensor(f"agw_in{i}", [1, ROWW], BF16).ap() for i in range(3)]
    agw_out = [nc.dram_tensor(f"agw_out{i}", [8, ROWW], BF16).ap() for i in range(3)]
    agl_in = nc.dram_tensor("agl_in", [1, ROWL], BF16).ap()
    agl_out = nc.dram_tensor("agl_out", [8, ROWL], BF16).ap()
    a2a_in = nc.dram_tensor("a2a_in", [8, A2AB], BF16).ap()
    a2a_out = nc.dram_tensor("a2a_out", [8, A2AB], BF16).ap()

    dbg = {}
    if DBG:
        for nm, shp, dt in [("dbg_resid1", [128, KD * TOK], F32),
                            ("dbg_ln2", [128, KD * TOK], BF16),
                            ("dbg_agl", [8, ROWL], BF16),
                            ("dbg_qn0", [128, S_ALL], BF16),
                            ("dbg_qrp", [128, S_ALL], BF16),
                            ("dbg_kn", [128, S_ALL], BF16),
                            ("dbg_v", [128, 64 * 132], BF16),
                            ("dbg_a2a", [8, A2AB], BF16),
                            ("dbg_resid2", [128, KD * TOK], F32),
                            ("dbg_qw", [8, ROWW], BF16),
                            ("dbg_lnx", [128, KD * TOK], BF16),
                            ("dbg_h", [128, HT * 512], BF16),
                            ("dbg_gthr", [128, 8], F32),
                            ("dbg_redc", [128, 32], F32)]:
            dbg[nm] = nc.dram_tensor(nm, shp, dt, kind="ExternalOutput").ap()

    SBASE = 16512
    STOP = 229344
    off = [0]

    def sat(name, shape, dt, at=None):
        a = off[0] if at is None else at
        th = nc.alloc_sbuf_tensor_at(name, shape, dt, offset=SBASE + a)
        esz = 2 if dt == BF16 else 4
        end = a + int(np.prod(shape[1:])) * esz
        assert SBASE + end <= STOP, f"{name} overflows sbuf: {SBASE+end}"
        if at is None:
            off[0] = (end + 31) // 32 * 32
        return th

    resid = sat("resid", [128, KD * TOK], F32)
    lnbuf = sat("lnbuf", [128, KD * TOK], BF16)
    onesr = sat("onesr_sb", [1, 128], F32)
    onesrow_b = sat("onesrow_b", [1, 256], BF16)
    onescol_f = sat("onescol_f", [128, 8], F32)
    id_f = sat("id_f", [128, 128], F32)
    id_b = sat("id_b", [128, 128], BF16)
    pmT = sat("pmT_sb", [128, 128], BF16)
    onecol_b = sat("onecol_b", [128, 16], BF16)
    qnw = sat("qnw_sb", [128, QKD], F32)
    kvnw = sat("kvnw_sb", [128, KVKD], F32)
    cell_ln = sat("cell_ln", [1, 8], F32)
    cell_rq = sat("cell_rq", [1, 8], F32)
    mcol = sat("mcol", [128, 16], F32)
    mbuf = sat("mbuf", [128, 8], F32)
    rlbuf = sat("rlbuf", [128, 8], F32)
    redc = sat("redc", [128, 32], F32)
    nmrow = sat("nmrow", [1, 128], BF16)
    m_row = sat("m_row", [1, 1024], F32)
    rstd_row = sat("rstd_row", [1, 1024], F32)
    srow = sat("srow", [1, 1024], F32)
    maskp = sat("maskp_sb", [128, 4 * 512], BF16)
    masktp = sat("masktp_sb", [128, 4 * 512], BF16)
    gthr = sat("gthr", [128, 8], F32)
    zcol = sat("zcol", [128, 8], F32)
    tmpf = sat("tmpf", [128, 512], F32)
    sqstage = sat("sqstage", [128, 1024], BF16)
    AREN = off[0]
    AREN_SZ = (STOP - SBASE) - AREN
    assert AREN_SZ >= 78848, f"arena too small: {AREN_SZ}"

    qch = [sat(f"qch{i}", [128, 1408], F32, at=AREN + i * 5632) for i in range(2)]
    qabs = sat("qabs", [128, 1408], F32, at=AREN + 11264)
    qsgn = sat("qsgn", [128, 1408], BF16, at=AREN + 16896)
    qq = [sat(f"qq{i}", [128, 1408], BF16, at=AREN + 19712 + i * 2816) for i in range(2)]
    h_buf = sat("h_buf", [128, HT * 512], BF16, at=AREN)
    qg1 = [sat(f"qg1_{i}", [128, 16 * 256], BF16, at=AREN + 45056 + i * 8192) for i in range(2)]
    qg2 = [sat(f"qg2_{i}", [128, 16 * 256], BF16, at=AREN + 61440 + i * 8192) for i in range(2)]
    q3b = [sat(f"q3b{i}", [128, HID], BF16, at=AREN + 45056 + i * 11264) for i in range(2)]
    sgstage = sat("sgstage", [128, 1024], BF16, at=AREN + 77824)
    qc_raw = sat("qc_raw", [128, QKD * TOK], BF16, at=AREN)
    kvc_raw = sat("kvc_raw", [128, KVKD * TOK], BF16, at=AREN + 24576)
    wdn_buf = [sat(f"wdn{i}", [128, 16 * 512], BF16, at=AREN + 32768 + i * 16384) for i in range(2)]
    wkr_buf = sat("wkr_buf", [128, 16 * 64], BF16, at=AREN + 65536)
    ck_sb = sat("ck_sb", [128, TOK], BF16, at=AREN + 67584)
    sk_sb = sat("sk_sb", [128, TOK], BF16, at=AREN + 69632)
    kr1 = sat("kr1", [128, 1024], BF16, at=AREN + 71680)
    kr2 = sat("kr2", [128, 1024], BF16, at=AREN + 73728)
    q_n = [sat(f"q_n{j}", [128, S_ALL], BF16, at=AREN + j * 16384) for j in range(2)]
    k_nope = sat("k_nope", [128, S_ALL], BF16, at=AREN + 32768)
    v_buf = sat("v_buf", [128, 64 * 132], BF16, at=AREN + 49152)
    kvk_sb = sat("kvk_sb", [128, KVKD * 256], BF16, at=AREN + 66048)
    kvv_sb = sat("kvv_sb", [128, KVKD * 256], BF16, at=AREN + 68096)
    kvwin = [sat(f"kvwin{i}", [128, KVKD * 512], BF16, at=AREN + 70144 + i * 4096) for i in range(2)]
    pT = [sat(f"pT{i}", [128, 512], BF16, at=AREN + 78336 + i * 1024) for i in range(2)]
    o_stage = sat("o_stage", [128, 1024], BF16, at=AREN + 80384)
    qcwin = [sat(f"qcwin{i}", [128, QKD * 512], BF16, at=AREN + 32768 + i * 12288) for i in range(2)]
    whead = sat("whead", [128, QKD * 384], BF16, at=AREN + 57344)
    tabc = [sat(f"tabc{i}", [128, 512], BF16, at=AREN + 66560 + i * 1024) for i in range(2)]
    tabs = [sat(f"tabs{i}", [128, 512], BF16, at=AREN + 68608 + i * 1024) for i in range(2)]
    qrps = sat("qrps", [128, 512], BF16, at=AREN + 70656)
    tabt = sat("tabt", [128, 512], BF16, at=AREN + 71680)
    LOFF = 65536
    krope_all = sat("krope_all", [128, S_ALL], BF16, at=LOFF)
    q_rp = sat("q_rp", [128, S_ALL], BF16, at=LOFF + 16384)
    wo_buf = [sat(f"wo_buf{i}", [128, 16 * 512], BF16, at=AREN + i * 16384) for i in range(2)]
    aow = [sat(f"aow{i}", [128, 16 * 512], BF16, at=AREN + 32768 + i * 16384) for i in range(2)]

    ctx = contextlib.ExitStack()
    pb = [ctx.enter_context(nc.psum_tensor(f"pb{i}", [128, 512], F32)) for i in range(7)]
    pbT = ctx.enter_context(nc.psum_tensor("pbT", [128, 1024], BF16))

    SY, SG, ST_, SS, SV, SC = "sy", "sg", "st", "ss", "sv", "sc"

    D("y", resid[:, :], xT.rearrange("(kd p) t -> p kd t", p=128), SY)
    D("y", onesr[0:1, :], onesr_in, SY)
    D("y", id_f[:, :], idf_in, SY)
    D("y", id_b[:, :], idb_in, SY)
    D("y", pmT[:, :], pmT_in, SY)
    D("y", qnw[:, :], qnw_t, SY)
    D("y", kvnw[:, :], kvnw_t, SY)
    D("y", maskp[:, :], maskp_in, SY)
    D("y", masktp[:, :], masktp_in, SY)
    E("g", lambda g: g.memset(onecol_b[:, :], 1.0))
    E("g", lambda g: g.memset(onesrow_b[:, :], 1.0))
    E("g", lambda g: g.memset(onescol_f[:, :], 1.0))
    E("g", lambda g: g.memset(cell_ln[:, :], 1e-5))
    E("g", lambda g: g.memset(cell_rq[:, :], 1e-6))
    E("g", lambda g: g.memset(zcol[:, :], 0.0), inc=SG)
    B.barrier()

    if PHASE_LIMIT >= 1:
        w12v = [w1_s.rearrange("(a p) h -> a p h", p=128),
                w2_s.rearrange("(a p) h -> a p h", p=128)]
        w3v = w3_s.rearrange("(a p) d -> p a d", p=128)
        agwv12 = [agw_in[i].rearrange("o (a p h) -> o a p h", a=2, p=128) for i in range(2)]
        agwv3 = agw_in[2].rearrange("o (a p d) -> o p a d", a=44, p=128)

        def chunk_src(mi, chn):
            if mi < 2:
                return w12v[mi][chn // 4, :, (chn % 4) * 1408:(chn % 4 + 1) * 1408]
            return w3v[:, 4 * chn:4 * chn + 4, :]

        NCH = [8, 8, 11]
        CW = [1408, 1408, 1024]
        ycons = []
        for mi in range(3):
            for chn in range(NCH[mi]):
                bi = chn % 2
                if ycons:
                    W("y", SV, ycons[-1])
                D("y", qch[bi][:, 0:CW[mi]], chunk_src(mi, chn), SY)
                W("v", SY)
                col = [0, 8, 16][mi] + chn
                E("v", (lambda bi_, col_, w_: lambda v: v.reduce_sum(
                    redc[:, col_:col_ + 1], qch[bi_][:, 0:w_], axis=AX.X,
                    apply_absolute_value=True))(bi, col, CW[mi]), inc=SV)
                ycons.append(M(SV))
        W("v", SV)
        for mi in range(3):
            E("v", (lambda mi_, n_: lambda v: v.reduce_sum(
                redc[:, 28 + mi_:29 + mi_], redc[:, [0, 8, 16][mi_]:[0, 8, 16][mi_] + n_], axis=AX.X))(mi, NCH[mi]),
              inc=SV)
        W("t", SV)
        E("t", lambda t: t.matmul(pb[6][0:1, 0:4], onescol_f[:, 0:1], redc[:, 28:32],
                                  start=True, stop=True), inc=ST_)
        W("v", ST_)
        E("v", lambda v: v.tensor_copy(srow[0:1, 0:4], pb[6][0:1, 0:4]), inc=SV)
        W("y", SV)
        D("y", g_in, srow[0:1, 0:4], SY)
        W("g", SY)
        E("g", lambda g: g.collective_compute(
            "AllReduce", ALU.add, replica_groups=[list(range(NCORE))],
            ins=[g_in.opt()], outs=[g_out.opt()]), inc=SC)
        W("y", SC)
        D("y", srow[0:1, 0:4], g_out, SY)
        W("v", SY)
        E("v", lambda v: v.tensor_scalar(srow[0:1, 4:8], srow[0:1, 0:4], -0.5 / WCNT, -0.5e-8,
                                         op0=ALU.mult, op1=ALU.add), inc=SV)
        E("v", lambda v: v.tensor_scalar(srow[0:1, 0:4], srow[0:1, 0:4], 0.5 / WCNT, 0.5e-8,
                                         op0=ALU.mult, op1=ALU.add), inc=SV)
        W("t", SV)
        E("t", lambda t: t.matmul(pb[6][:, 0:8], onesr[0:1, :], srow[0:1, 0:8],
                                  start=True, stop=True), inc=ST_)
        W("v", ST_)
        E("v", lambda v: v.tensor_copy(gthr[:, 0:8], pb[6][:, 0:8]), inc=SV)
        B.barrier()
        for mi in range(3):
            dmamarks = []
            vq_marks = []
            for chn in range(NCH[mi]):
                bi = chn % 2
                wd = CW[mi]
                if len(vq_marks) >= 2:
                    W("y", SV, vq_marks[-2])
                D("y", qch[bi][:, 0:wd], chunk_src(mi, chn), SY)
                if vq_marks:
                    W("s", SV, vq_marks[-1])
                W("s", SY)
                E("s", (lambda bi_, w_: lambda s: s.sign(qsgn[:, 0:w_], qch[bi_][:, 0:w_]))(bi, wd))
                E("s", (lambda bi_, w_: lambda s: s.activation(
                    qabs[:, 0:w_], qch[bi_][:, 0:w_], AF.Abs))(bi, wd), inc=SS)
                W("s", SS)
                E("s", (lambda w_, mi_: lambda s: s.sign(
                    qabs[:, 0:w_], qabs[:, 0:w_], bias=gthr[:, 4 + mi_:5 + mi_]))(wd, mi), inc=SS)
                if len(dmamarks) >= 2:
                    W("v", SY, dmamarks[-2])
                W("v", SS)
                E("v", (lambda w_: lambda v: v.tensor_relu(qabs[:, 0:w_], qabs[:, 0:w_]))(wd), inc=SV)
                W("v", SV)
                E("v", (lambda bi_, w_: lambda v: v.tensor_mul(
                    qq[bi_][:, 0:w_], qabs[:, 0:w_], qsgn[:, 0:w_]))(bi, wd), inc=SV)
                vq_marks.append(M(SV))
                W("y", SV)
                if mi < 2:
                    dst = agwv12[mi][0, chn // 4, :, (chn % 4) * 1408:(chn % 4 + 1) * 1408]
                    D("y", dst, qq[bi][:, 0:wd], SY)
                else:
                    dst = agwv3[0, :, 4 * chn:4 * chn + 4, :]
                    D("y", dst, qq[bi][:, 0:1024].rearrange("p (e d) -> p e d", e=4), SY)
                dmamarks.append(M(SY))
            W("g", SY)
            E("g", (lambda mi_: lambda g: g.collective_compute(
                "AllGather", ALU.bypass, replica_groups=[list(range(NCORE))],
                ins=[agw_in[mi_].opt()], outs=[agw_out[mi_].opt()]))(mi), inc=SC)
        B.barrier()

    def ln_phase():
        for kd in range(KD):
            W("v", ST_)
            E("v", (lambda kd_: lambda v: v.tensor_copy(
                lnbuf[:, kd_ * 1024:(kd_ + 1) * 1024], resid[:, kd_ * 1024:(kd_ + 1) * 1024]))(kd), inc=SV)
            W("v", SV)
            E("v", (lambda kd_: lambda v: v.tensor_mul(
                sqstage[:, :], lnbuf[:, kd_ * 1024:(kd_ + 1) * 1024],
                lnbuf[:, kd_ * 1024:(kd_ + 1) * 1024]))(kd), inc=SV)
            W("t", SV)
            st0, sp0 = (kd == 0), (kd == KD - 1)
            for half in range(2):
                E("t", (lambda kd_, h_, s0, s1: lambda t: t.matmul(
                    pb[2 + h_][0:1, 0:512], onecol_b[:, 0:1],
                    lnbuf[:, kd_ * 1024 + h_ * 512:kd_ * 1024 + h_ * 512 + 512],
                    start=s0, stop=s1))(kd, half, st0, sp0))
                E("t", (lambda h_, s0, s1: lambda t: t.matmul(
                    pb[4 + h_][0:1, 0:512], onecol_b[:, 0:1],
                    sqstage[:, h_ * 512:h_ * 512 + 512],
                    start=s0, stop=s1))(half, st0, sp0), inc=(ST_ if (half == 1) else None))
        W("v", ST_)
        E("v", lambda v: v.tensor_scalar(m_row[0:1, 0:512], pb[2][0:1, 0:512], 1.0 / DM, None, op0=ALU.mult))
        E("v", lambda v: v.tensor_scalar(m_row[0:1, 512:1024], pb[3][0:1, 0:512], 1.0 / DM, None, op0=ALU.mult))
        E("v", lambda v: v.tensor_scalar(srow[0:1, 0:512], pb[4][0:1, 0:512], 1.0 / DM, None, op0=ALU.mult))
        E("v", lambda v: v.tensor_scalar(srow[0:1, 512:1024], pb[5][0:1, 0:512], 1.0 / DM, None, op0=ALU.mult), inc=SV)
        W("v", SV)
        E("v", lambda v: v.tensor_mul(rstd_row[0:1, :], m_row[0:1, :], m_row[0:1, :]), inc=SV)
        W("v", SV)
        E("v", lambda v: v.tensor_sub(rstd_row[0:1, :], srow[0:1, :], rstd_row[0:1, :]), inc=SV)
        W("s", SV)
        E("s", lambda s: s.activation(rstd_row[0:1, :], rstd_row[0:1, :], AF.Ln, bias=cell_ln[0:1, 0:1]))
        E("s", lambda s: s.activation(rstd_row[0:1, :], rstd_row[0:1, :], AF.Exp, scale=-0.5), inc=SS)
        for half in range(2):
            W("t", SS)
            if half == 1:
                W("t", SV)
            E("t", (lambda h_: lambda t: t.matmul(
                pb[6][:, 0:512], onesr[0:1, :], m_row[0:1, h_ * 512:h_ * 512 + 512],
                start=True, stop=True))(half))
            E("t", (lambda h_: lambda t: t.matmul(
                pb[0][:, 0:512], onesr[0:1, :], rstd_row[0:1, h_ * 512:h_ * 512 + 512],
                start=True, stop=True))(half), inc=ST_)
            W("v", ST_)
            for kd in range(KD):
                c0 = kd * 1024 + half * 512
                E("v", (lambda c_: lambda v: v.tensor_sub(
                    tmpf[:, :], resid[:, c_:c_ + 512], pb[6][:, 0:512]))(c0), inc=SV)
                W("v", SV)
                E("v", (lambda c_: lambda v: v.tensor_mul(
                    lnbuf[:, c_:c_ + 512], tmpf[:, :], pb[0][:, 0:512]))(c0), inc=SV)
        B.barrier()

    def ffn_phase(first):
        agq1 = agw_out[0].rearrange("r (a p h) -> p r a h", a=2, p=128)
        agq2 = agw_out[1].rearrange("r (a p h) -> p r a h", a=2, p=128)
        agq3 = agw_out[2].rearrange("r (a p d) -> r p a d", a=44, p=128)
        if first:
            W("y", SC)
        for half in range(2):
            hoff = half * 512
            sig_marks = []
            hv_marks = []
            for mg in range(22):
                bi = mg % 2
                if mg == 0:
                    W("y", ST_, M(ST_))
                if mg >= 2:
                    W("y", ST_, qg_cons[bi])
                D("y", qg1[bi][:, :].rearrange("p (r a hh) -> p r a hh", r=8, a=2),
                  agq1[:, :, :, mg * 256:(mg + 1) * 256], f"yq{bi}")
                D("y", qg2[bi][:, :].rearrange("p (r a hh) -> p r a hh", r=8, a=2),
                  agq2[:, :, :, mg * 256:(mg + 1) * 256], f"yq{bi}")
                W("t", f"yq{bi}")
                for mt in range(2):
                    ht = mg * 2 + mt
                    pA = pb[ht % 2]
                    pB = pb[2 + ht % 2]
                    if ht >= 2:
                        W("t", SS, sig_marks[ht - 2])
                        W("t", SV, hv_marks[ht - 2])
                    for kdi in range(KD):
                        E("t", (lambda buf_, k_, m_, pA_, h0_: lambda t: t.matmul(
                            pA_[:, 0:512], buf_[:, k_ * 256 + m_ * 128:k_ * 256 + m_ * 128 + 128],
                            lnbuf[:, k_ * 1024 + h0_:k_ * 1024 + h0_ + 512],
                            start=(k_ == 0), stop=(k_ == KD - 1)))(qg1[bi], kdi, mt, pA, hoff),
                          inc=(ST_ if kdi == KD - 1 else None))
                    for kdi in range(KD):
                        E("t", (lambda buf_, k_, m_, pB_, h0_: lambda t: t.matmul(
                            pB_[:, 0:512], buf_[:, k_ * 256 + m_ * 128:k_ * 256 + m_ * 128 + 128],
                            lnbuf[:, k_ * 1024 + h0_:k_ * 1024 + h0_ + 512],
                            start=(k_ == 0), stop=(k_ == KD - 1)))(qg2[bi], kdi, mt, pB, hoff),
                          inc=(ST_ if kdi == KD - 1 else None))
                    W("s", ST_, M(ST_) - 1)
                    if ht >= 2:
                        W("s", SV, hv_marks[ht - 2])
                    E("s", (lambda pA_, ht_: lambda s: s.activation(
                        sgstage[:, (ht_ % 2) * 512:(ht_ % 2) * 512 + 512],
                        pA_[:, 0:512], AF.Sigmoid))(pA, ht), inc=SS)
                    sig_marks.append(M(SS))
                    W("v", SS)
                    W("v", ST_)
                    E("v", (lambda pB_, ht_: lambda v: v.tensor_mul(
                        h_buf[:, ht_ * 512:ht_ * 512 + 512],
                        sgstage[:, (ht_ % 2) * 512:(ht_ % 2) * 512 + 512], pB_[:, 0:512]))(pB, ht),
                      inc=SV)
                    hv_marks.append(M(SV))
                if mg == 0:
                    qg_cons = [None, None]
                qg_cons[bi] = M(ST_)
            h_done = M(SV)
            x12_done = M(ST_)
            q3_cons = [None, None]
            res_marks = []
            for dt in range(KD):
                bi = dt % 2
                if dt == 0:
                    W("y", ST_, x12_done)
                if dt >= 2:
                    W("y", ST_, q3_cons[bi])
                D("y", q3b[bi][:, :].rearrange("p (a dd) -> p a dd", a=44),
                  agq3[dt // 2, :, :, (dt % 2) * 128:(dt % 2) * 128 + 128], f"y3{bi}")
                W("t", f"y3{bi}")
                if dt == 0:
                    W("t", SV, h_done)
                pC = pb[4 + dt % 2]
                if dt >= 2:
                    W("t", SV, res_marks[dt - 2])
                for a in range(HT):
                    E("t", (lambda b_, a_, pC_: lambda t: t.matmul(
                        pC_[:, 0:512], q3b[b_][:, a_ * 128:a_ * 128 + 128],
                        h_buf[:, a_ * 512:a_ * 512 + 512],
                        start=(a_ == 0), stop=(a_ == HT - 1)))(bi, a, pC),
                      inc=(ST_ if a == HT - 1 else None))
                q3_cons[bi] = M(ST_)
                W("v", ST_)
                E("v", (lambda pC_, c_: lambda v: v.scalar_tensor_tensor(
                    resid[:, c_:c_ + 512], pC_[:, 0:512], 0.5, resid[:, c_:c_ + 512],
                    op0=ALU.mult, op1=ALU.add))(pC, dt * 1024 + hoff), inc=SV)
                res_marks.append(M(SV))
        B.barrier()

    if PHASE_LIMIT >= 2:
        ln_phase()
        if DBG:
            D("y", dbg["dbg_lnx"], lnbuf[:, :], SY)
            B.barrier()
        ffn_phase(first=True)
        if DBG:
            D("y", dbg["dbg_resid1"], resid[:, :], SY)
            W("y", SC)
            D("y", dbg["dbg_qw"], agw_out[0], SY)
            D("y", dbg["dbg_gthr"], gthr[:, :], SY)
            D("y", dbg["dbg_redc"], redc[:, :], SY)
            D("y", dbg["dbg_h"], h_buf[:, :], SY)
    if PHASE_LIMIT >= 3:
        ln_phase()
        if DBG:
            D("y", dbg["dbg_ln2"], lnbuf[:, :], SY)
            B.barrier()

    if PHASE_LIMIT >= 4:
        wqdv = wq_down.rearrange("(kd p) m -> p kd m", p=128)
        wkvdv = wkv_down.rearrange("(kd p) m -> p kd m", p=128)
        wkrv = wk_rope.rearrange("(kd p) m -> p kd m", p=128)
        aglq = agl_in.rearrange("o (a p t) -> o a p t", a=QR // 128 + KVR // 128 + 1, p=128)

        def proj_rms(nt, wview, mgroups, statbank, meann, rawbuf):
            wdn_cons = [0, 0]
            W("v", SY)
            for mg in range(mgroups):
                bi = mg % 2
                if mg >= 2:
                    W("g", ST_, wdn_cons[bi])
                D("g", wdn_buf[bi][:, :].rearrange("p (kd mm) -> p kd mm", kd=16),
                  wview[:, :, mg * 512:(mg + 1) * 512], f"gw{bi}")
                W("t", f"gw{bi}")
                for mt in range(4):
                    tile = mg * 4 + mt
                    if tile >= nt:
                        break
                    for half in range(2):
                        pq = pb[(tile * 2 + half) % 2]
                        if tile * 2 + half >= 2:
                            W("t", SS, M(SS) - 1)
                        for kdi in range(KD):
                            E("t", (lambda b_, k_, m_, p_, h_: lambda t: t.matmul(
                                p_[:, 0:512], wdn_buf[b_][:, k_ * 512 + m_ * 128:k_ * 512 + m_ * 128 + 128],
                                lnbuf[:, k_ * 1024 + h_ * 512:k_ * 1024 + h_ * 512 + 512],
                                start=(k_ == 0), stop=(k_ == KD - 1)))(bi, kdi, mt, pq, half),
                              inc=(ST_ if kdi == KD - 1 else None))
                        W("s", ST_)
                        E("s", (lambda p_, c_: lambda s: s.copy(
                            rawbuf[:, c_:c_ + 512], p_[:, 0:512]))(pq, tile * 1024 + half * 512), inc=SS)
                        W("v", SS)
                        if tile > 0 or half > 0:
                            W("v", ST_)
                        E("v", (lambda c_, h_: lambda v: v.tensor_mul(
                            sqstage[:, h_ * 512:h_ * 512 + 512], rawbuf[:, c_:c_ + 512],
                            rawbuf[:, c_:c_ + 512]))(tile * 1024 + half * 512, half), inc=SV)
                        W("t", SV)
                        E("t", (lambda h_, s0, s1, sb_: lambda t: t.matmul(
                            pb[sb_ + h_][0:1, 0:512], onecol_b[:, 0:1],
                            sqstage[:, h_ * 512:h_ * 512 + 512],
                            start=s0, stop=s1))(half, tile == 0, tile == nt - 1, statbank),
                          inc=(ST_ if (tile == nt - 1) else None))
                        wdn_cons[bi] = M(ST_) if B.cnt.get(ST_) else wdn_cons[bi]
            W("s", ST_)
            E("s", (lambda sb_, mn_: lambda s: s.activation(
                rstd_row[0:1, 0:512], pb[sb_][0:1, 0:512], AF.Ln, bias=cell_rq[0:1, 0:1],
                scale=1.0 / mn_))(statbank, meann))
            E("s", (lambda sb_, mn_: lambda s: s.activation(
                rstd_row[0:1, 512:1024], pb[sb_ + 1][0:1, 0:512], AF.Ln, bias=cell_rq[0:1, 0:1],
                scale=1.0 / mn_))(statbank, meann))
            E("s", lambda s: s.activation(rstd_row[0:1, :], rstd_row[0:1, :], AF.Exp, scale=-0.5), inc=SS)
            for half in range(2):
                W("t", SS)
                if half == 1:
                    W("t", SY)
                E("t", (lambda h_: lambda t: t.matmul(
                    pb[6][:, 0:512], onesr[0:1, :], rstd_row[0:1, h_ * 512:h_ * 512 + 512],
                    start=True, stop=True))(half), inc=ST_)
                W("v", ST_)
                for tile in range(nt):
                    stg = sqstage[:, (tile % 2) * 512:(tile % 2) * 512 + 512]
                    if tile >= 2:
                        W("v", SY, M(SY) - 16)
                    E("v", (lambda stg_, c_: lambda v: v.tensor_mul(
                        stg_, rawbuf[:, c_:c_ + 512], pb[6][:, 0:512]))(stg, tile * 1024 + half * 512),
                      inc=SV)
                    W("y", SV)
                    D("y", (lambda t_, h_: aglq[0, t_ + (0 if rawbuf is qc_raw else QR // 128), :,
                                                h_ * 512:h_ * 512 + 512])(tile, half), stg, SY)

        proj_rms(QKD, wqdv, 3, 2, QR, qc_raw)
        B.barrier()
        proj_rms(KVKD, wkvdv, 1, 2, KVR, kvc_raw)
        B.barrier()
        D("y", ck_sb[:, :], ropeck, SY)
        D("y", sk_sb[:, :], ropesk, SY)
        D("g", wkr_buf[:, :].rearrange("p (kd mm) -> p kd mm", kd=16), wkrv, SG)
        W("t", SG)
        for half in range(2):
            for base in (0, 64):
                for kdi in range(KD):
                    E("t", (lambda k_, b_, h_: lambda t: t.matmul(
                        pb[1][b_:b_ + 64, 0:512], wkr_buf[:, k_ * 64:k_ * 64 + 64],
                        lnbuf[:, k_ * 1024 + h_ * 512:k_ * 1024 + h_ * 512 + 512],
                        start=(k_ == 0), stop=(k_ == KD - 1)))(kdi, base, half),
                      inc=(ST_ if kdi == KD - 1 else None))
            W("s", ST_)
            E("s", (lambda h_: lambda s: s.copy(kr1[:, h_ * 512:h_ * 512 + 512], pb[1][:, 0:512]))(half),
              inc=SS)
            W("t", SS)
            E("t", (lambda h_: lambda t: t.matmul(
                pb[0][:, 0:512], pmT[:, :], kr1[:, h_ * 512:h_ * 512 + 512],
                start=True, stop=True))(half), inc=ST_)
            W("v", SS)
            E("v", (lambda h_: lambda v: v.tensor_mul(
                kr2[:, h_ * 512:h_ * 512 + 512], kr1[:, h_ * 512:h_ * 512 + 512],
                ck_sb[:, h_ * 512:h_ * 512 + 512]))(half))
            W("v", ST_)
            E("v", (lambda h_: lambda v: v.tensor_mul(
                kr1[:, h_ * 512:h_ * 512 + 512], pb[0][:, 0:512],
                sk_sb[:, h_ * 512:h_ * 512 + 512]))(half), inc=SV)
            W("v", SV)
            E("v", (lambda h_: lambda v: v.tensor_add(
                kr2[:, h_ * 512:h_ * 512 + 512], kr2[:, h_ * 512:h_ * 512 + 512],
                kr1[:, h_ * 512:h_ * 512 + 512]))(half), inc=SV)
        W("y", SV)
        D("y", aglq[0, QR // 128 + KVR // 128, :, :], kr2[:, :], SY)
        W("g", SY)
        E("g", lambda g: g.collective_compute(
            "AllGather", ALU.bypass, replica_groups=[list(range(NCORE))],
            ins=[agl_in.opt()], outs=[agl_out.opt()]), inc=SC)
        B.barrier()
        if DBG:
            W("y", SC)
            D("y", dbg["dbg_agl"], agl_out, SY)
            B.barrier()

    if PHASE_LIMIT >= 5:
        W("y", SC)
        aglqv = agl_out[:, 0:QOFF].rearrange("r (a p t) -> r p a t", p=128, t=1024)
        aglkv = agl_out[:, QOFF:KVOFF].rearrange("r (a p t) -> r p a t", p=128, t=1024)
        aglkr = agl_out[:, KVOFF:ROWL].rearrange("r (p t) -> p r t", p=128)
        D("g", whead[:, :].rearrange("p (kd mm) -> p kd mm", kd=QKD),
          wq_head.rearrange("(kd p) m -> p kd m", p=128), SG)
        W("v", SG)
        for kdi in range(QKD):
            E("v", (lambda k_: lambda v: v.tensor_scalar(
                whead[:, k_ * 384:k_ * 384 + 384], whead[:, k_ * 384:k_ * 384 + 384],
                qnw[:, k_:k_ + 1], None, op0=ALU.mult))(kdi), inc=(SV if kdi == QKD - 1 else None))
        D("y", krope_all[:, :].rearrange("p (r t) -> p r t", r=8), aglkr, SY)
        W("t", SV)
        for w in range(16):
            bi = w % 2
            if w >= 2:
                W("y", SV, M(SV))
            D("y", qcwin[bi][:, :].rearrange("p (a tt) -> p a tt", a=QKD),
              aglqv[w // 2, :, :, (w % 2) * 512:(w % 2) * 512 + 512], f"yc{bi}")
            D("y", tabc[bi][:, :], ropecq[:, w * 512:(w + 1) * 512], f"yc{bi}")
            D("y", tabs[bi][:, :], ropesq[:, w * 512:(w + 1) * 512], f"yc{bi}")
            W("t", f"yc{bi}")
            W("v", f"yc{bi}")
            for (mi, pdst) in ((0, pb[0]), (1, pb[1]), (2, pb[2])):
                for kdi in range(QKD):
                    E("t", (lambda b_, k_, m_, p_: lambda t: t.matmul(
                        p_[:, 0:512], whead[:, k_ * 384 + m_ * 128:k_ * 384 + m_ * 128 + 128],
                        qcwin[b_][:, k_ * 512:k_ * 512 + 512],
                        start=(k_ == 0), stop=(k_ == QKD - 1)))(bi, kdi, mi, pdst),
                      inc=(ST_ if kdi == QKD - 1 else None))
            W("s", ST_)
            E("s", (lambda w_: lambda s: s.activation(
                q_n[0][:, w_ * 512:w_ * 512 + 512], pb[0][:, 0:512], AF.Copy, scale=SCALE))(w))
            E("s", (lambda w_: lambda s: s.activation(
                q_n[1][:, w_ * 512:w_ * 512 + 512], pb[1][:, 0:512], AF.Copy, scale=SCALE))(w))
            E("s", lambda s: s.copy(qrps[:, :], pb[2][:, 0:512]), inc=SS)
            W("t", SS)
            E("t", lambda t: t.matmul(pb[3][:, 0:512], pmT[:, :], qrps[:, :],
                                      start=True, stop=True), inc=ST_)
            W("v", SS)
            E("v", (lambda b_: lambda v: v.tensor_mul(tabt[:, :], qrps[:, :], tabc[b_][:, :]))(bi))
            W("v", ST_)
            E("v", (lambda b_: lambda v: v.tensor_mul(qrps[:, :], pb[3][:, 0:512], tabs[b_][:, :]))(bi), inc=SV)
            W("v", SV)
            E("v", (lambda w_: lambda v: v.tensor_add(
                q_rp[:, w_ * 512:w_ * 512 + 512], tabt[:, :], qrps[:, :]))(w), inc=SV)
        B.barrier()
        if DBG:
            D("y", dbg["dbg_qn0"], q_n[0][:, :], SY)
            D("y", dbg["dbg_qrp"], q_rp[:, :], SY)
            B.barrier()

        D("g", kvk_sb[:, :].rearrange("p (kd mm) -> p kd mm", kd=KVKD),
          wkv_k.rearrange("(kd p) m -> p kd m", p=128), SG)
        D("g", kvv_sb[:, :].rearrange("p (kd mm) -> p kd mm", kd=KVKD),
          wkv_v.rearrange("(kd p) m -> p kd m", p=128), SG)
        W("v", SG)
        for kdi in range(KVKD):
            E("v", (lambda k_: lambda v: v.tensor_scalar(
                kvk_sb[:, k_ * 256:k_ * 256 + 256], kvk_sb[:, k_ * 256:k_ * 256 + 256],
                kvnw[:, k_:k_ + 1], None, op0=ALU.mult))(kdi))
            E("v", (lambda k_: lambda v: v.tensor_scalar(
                kvv_sb[:, k_ * 256:k_ * 256 + 256], kvv_sb[:, k_ * 256:k_ * 256 + 256],
                kvnw[:, k_:k_ + 1], None, op0=ALU.mult))(kdi), inc=(SV if kdi == KVKD - 1 else None))
        B.barrier()

        o_dma_mark = [None]
        for j in range(2):
            if j == 0:
                E("g", lambda g: g.memset(v_buf[:, :], 1.0), inc=SG)
                W("s", SG)
            for w in range(16):
                bi = w % 2
                if w >= 2:
                    W("y", ST_, M(ST_))
                D("y", kvwin[bi][:, :].rearrange("p (a tt) -> p a tt", a=KVKD),
                  aglkv[w // 2, :, :, (w % 2) * 512:(w % 2) * 512 + 512], f"yk{bi}")
                W("t", f"yk{bi}")
                for kdi in range(KVKD):
                    E("t", (lambda b_, k_, j_: lambda t: t.matmul(
                        pb[0][:, 0:512], kvk_sb[:, k_ * 256 + j_ * 128:k_ * 256 + j_ * 128 + 128],
                        kvwin[b_][:, k_ * 512:k_ * 512 + 512],
                        start=(k_ == 0), stop=(k_ == KVKD - 1)))(bi, kdi, j),
                      inc=(ST_ if kdi == KVKD - 1 else None))
                W("s", ST_)
                E("s", (lambda w_: lambda s: s.copy(
                    k_nope[:, w_ * 512:w_ * 512 + 512], pb[0][:, 0:512]))(w), inc=SS)
                for tt in range(4):
                    pv_ = pb[2 + tt % 2]
                    if w > 0 or tt >= 2:
                        W("t", SS, M(SS) - 1)
                    for kdi in range(KVKD):
                        E("t", (lambda b_, k_, t2, j_, p_: lambda t: t.matmul(
                            p_[:, 0:128], kvwin[b_][:, k_ * 512 + t2 * 128:k_ * 512 + t2 * 128 + 128],
                            kvv_sb[:, k_ * 256 + j_ * 128:k_ * 256 + j_ * 128 + 128],
                            start=(k_ == 0), stop=(k_ == KVKD - 1)))(bi, kdi, tt, j, pv_),
                          inc=(ST_ if kdi == KVKD - 1 else None))
                    W("s", ST_)
                    E("s", (lambda p_, c_: lambda s: s.copy(
                        v_buf[:, c_:c_ + 128], p_[:, 0:128]))(pv_, (4 * w + tt) * 132), inc=SS)
                W("t", SS, M(SS))
            B.barrier()
            if DBG and j == 0:
                D("y", dbg["dbg_kn"], k_nope[:, :], SY)
                D("y", dbg["dbg_v"], v_buf[:, :], SY)
                B.barrier()

            for qt in range(64):
                R = qt // 4 + 1
                ph = qt % 4
                red_marks = []
                for kc in range(R):
                    sb_ = pb[kc % 2]
                    if kc >= 2:
                        W("t", SV, red_marks[kc - 2])
                    E("t", (lambda j_, q_, k_, p_: lambda t: t.matmul(
                        p_[:, 0:512], q_n[j_][:, q_ * 128:q_ * 128 + 128],
                        k_nope[:, k_ * 512:k_ * 512 + 512], start=True, stop=False))(j, qt, kc, sb_))
                    E("t", (lambda j_, q_, k_, p_: lambda t: t.matmul(
                        p_[:, 0:512], q_rp[64 * j_:64 * j_ + 64, q_ * 128:q_ * 128 + 128],
                        krope_all[64 * j_:64 * j_ + 64, k_ * 512:k_ * 512 + 512],
                        start=False, stop=True))(j, qt, kc, sb_), inc=ST_)
                    W("v", ST_)
                    if kc == R - 1:
                        E("v", (lambda p_, ph_: lambda v: v.tensor_add(
                            p_[:, 0:512], p_[:, 0:512], maskp[:, ph_ * 512:ph_ * 512 + 512]))(sb_, ph),
                          inc=SV)
                        W("v", SV)
                    E("v", (lambda p_, k_: lambda v: v.reduce_max(
                        mcol[:, k_:k_ + 1], p_[:, 0:512], axis=AX.X))(sb_, kc), inc=SV)
                    red_marks.append(M(SV))
                W("v", SV)
                E("v", (lambda r_: lambda v: v.reduce_max(mbuf[:, 0:1], mcol[:, 0:r_], axis=AX.X))(R),
                  inc=SV)
                W("t", SV)
                E("t", lambda t: t.transpose(pb[6][0:1, 0:128], mbuf[:, 0:1], id_f[:, :]), inc=ST_)
                W("s", ST_)
                E("s", lambda s: s.activation(nmrow[0:1, :], pb[6][0:1, 0:128], AF.Copy, scale=-1.0),
                  inc=SS)
                nm_mark = M(SS)
                exp_marks = []
                Ob = pb[4 + qt % 2]
                for kc in range(R):
                    stb = pb[2 + kc % 2]
                    W("t", SS, nm_mark)
                    if kc >= 2:
                        W("t", SS, exp_marks[kc - 2])
                    for ks in range(4):
                        E("t", (lambda j_, q_, k_, ks_, p_: lambda t: t.matmul(
                            p_[:, ks_ * 128:ks_ * 128 + 128],
                            k_nope[:, k_ * 512 + ks_ * 128:k_ * 512 + ks_ * 128 + 128],
                            q_n[j_][:, q_ * 128:q_ * 128 + 128], start=True, stop=False))(j, qt, kc, ks, stb))
                        E("t", (lambda j_, q_, k_, ks_, p_: lambda t: t.matmul(
                            p_[:, ks_ * 128:ks_ * 128 + 128],
                            krope_all[64 * j_:64 * j_ + 64, k_ * 512 + ks_ * 128:k_ * 512 + ks_ * 128 + 128],
                            q_rp[64 * j_:64 * j_ + 64, q_ * 128:q_ * 128 + 128],
                            start=False, stop=False))(j, qt, kc, ks, stb))
                        E("t", (lambda ks_, p_: lambda t: t.matmul(
                            p_[:, ks_ * 128:ks_ * 128 + 128], onesrow_b[0:1, 0:128],
                            nmrow[0:1, 0:128], start=False, stop=True))(ks, stb),
                          inc=(ST_ if ks == 3 else None))
                    if kc == R - 1:
                        W("v", ST_)
                        E("v", (lambda p_, ph_: lambda v: v.tensor_add(
                            p_[:, 0:512], p_[:, 0:512], masktp[:, ph_ * 512:ph_ * 512 + 512]))(stb, ph),
                          inc=SV)
                        W("s", SV)
                    else:
                        W("s", ST_)
                    E("s", (lambda p_, b_: lambda s: s.activation(
                        pT[b_][:, :], p_[:, 0:512], AF.Exp))(stb, kc % 2), inc=SS)
                    exp_marks.append(M(SS))
                    W("t", SS, exp_marks[-1])
                    for ks in range(4):
                        E("t", (lambda b_, k_, ks_, O_: lambda t: t.matmul(
                            O_[:, 0:132], pT[b_][:, ks_ * 128:ks_ * 128 + 128],
                            v_buf[:, (k_ * 4 + ks_) * 132:(k_ * 4 + ks_) * 132 + 132],
                            start=(k_ == 0 and ks_ == 0), stop=(k_ == R - 1 and ks_ == 3)))(kc % 2, kc, ks, Ob),
                          inc=(ST_ if (kc == R - 1 and ks == 3) else None))
                W("v", ST_)
                E("v", (lambda O_: lambda v: v.reciprocal(rlbuf[:, 0:1], O_[:, 128:129]))(Ob), inc=SV)
                W("s", SV)
                E("s", (lambda O_: lambda s: s.activation(
                    pT[0][:, 0:128], O_[:, 0:128], AF.Copy, scale=rlbuf[:, 0:1]))(Ob), inc=SS)
                W("t", SS)
                E("t", lambda t: t.transpose(pbT[:, 0:128], pT[0][:, 0:128], id_b[:, :]), inc=ST_)
                W("s", ST_)
                if qt % 8 == 0 and o_dma_mark[0] is not None:
                    W("s", SY, o_dma_mark[0])
                E("s", (lambda q_: lambda s: s.copy(
                    o_stage[:, (q_ % 8) * 128:(q_ % 8) * 128 + 128], pbT[:, 0:128]))(qt), inc=SS)
                if qt % 8 == 7:
                    W("y", SS)
                    D("y", a2a_in[qt // 8:qt // 8 + 1,
                                  j * 131072:(j + 1) * 131072].rearrange("o (p t) -> o p t", p=128)[0],
                      o_stage[:, :], SY)
                    o_dma_mark[0] = M(SY)
            B.barrier()
        W("g", SY)
        E("g", lambda g: g.collective_compute(
            "AllToAll", ALU.bypass, replica_groups=[list(range(NCORE))],
            ins=[a2a_in.opt()], outs=[a2a_out.opt()]), inc=SC)
        B.barrier()
        if DBG:
            W("y", SC)
            D("y", dbg["dbg_a2a"], a2a_out, SY)
            B.barrier()

    if PHASE_LIMIT >= 6:
        W("y", SC)
        a2av = a2a_out.rearrange("r (jj p t) -> p r jj t", jj=2, p=128)
        wov = wo_in.rearrange("(kv p) m -> p kv m", p=128)
        for half in range(2):
            D("y", aow[half][:, :].rearrange("p (r jj tt) -> p r jj tt", r=8, jj=2),
              a2av[:, :, :, half * 512:half * 512 + 512], SY)
        for mg in range(4):
            bi = mg % 2
            if mg >= 2:
                W("g", ST_, M(ST_))
            D("g", wo_buf[bi][:, :].rearrange("p (kv mm) -> p kv mm", kv=16),
              wov[:, :, mg * 512:(mg + 1) * 512], f"go{bi}")
            W("t", f"go{bi}")
            W("t", SY)
            for mt in range(4):
                dt = mg * 4 + mt
                for half in range(2):
                    pw = pb[(dt * 2 + half) % 2]
                    if dt * 2 + half >= 2:
                        W("t", SV, M(SV) - 1)
                    for kv in range(KD):
                        E("t", (lambda b_, k_, m_, h_, p_: lambda t: t.matmul(
                            p_[:, 0:512], wo_buf[b_][:, k_ * 512 + m_ * 128:k_ * 512 + m_ * 128 + 128],
                            aow[h_][:, k_ * 512:k_ * 512 + 512],
                            start=(k_ == 0), stop=(k_ == KD - 1)))(bi, kv, mt, half, pw),
                          inc=(ST_ if kv == KD - 1 else None))
                    W("v", ST_)
                    E("v", (lambda p_, c_: lambda v: v.scalar_tensor_tensor(
                        resid[:, c_:c_ + 512], p_[:, 0:512], 1.0, resid[:, c_:c_ + 512],
                        op0=ALU.mult, op1=ALU.add))(pw, dt * 1024 + half * 512), inc=SV)
        B.barrier()
        if DBG:
            D("y", dbg["dbg_resid2"], resid[:, :], SY)
            B.barrier()

    if PHASE_LIMIT >= 7:
        ln_phase()
        ffn_phase(first=False)

    D("y", yout.rearrange("(kd p) t -> p kd t", p=128), resid[:, :], SY)
    B.barrier()
    W("y", SY)

    with ctx:
        with nc.Block() as block:
            with contextlib.ExitStack() as semctx:
                for sname in B.cnt:
                    B.sems[sname] = semctx.enter_context(nc.semaphore(sname))
                with allow_ici_all_to_all():
                    B.replay(block)
    return nc


def make_inputs(full, core):
    c = core
    h0, h1 = 2 * c, 2 * c + 1
    x = np.asarray(full["L0_attn_x"])[0]
    w1 = np.asarray(full["L0_ffn_w1"]); w2 = np.asarray(full["L0_ffn_w2"]); w3 = np.asarray(full["L0_ffn_w3"])
    wqd = np.asarray(full["L0_attn_wq_down"]); qnw_ = np.asarray(full["L0_attn_q_norm_w"])
    wqu = np.asarray(full["L0_attn_wq_up"]); wqr = np.asarray(full["L0_attn_wq_rope"])
    wkvd = np.asarray(full["L0_attn_wkv_down"]); kvnw_ = np.asarray(full["L0_attn_kv_norm_w"])
    wkvu = np.asarray(full["L0_attn_wkv_up"]); wkr = np.asarray(full["L0_attn_wk_rope"])
    wo = np.asarray(full["L0_attn_wo"])
    bf = ml_dtypes.bfloat16
    C = np.ascontiguousarray

    pos = np.arange(S_ALL, dtype=np.float64)
    inv = 1.0 / 10000.0 ** (np.arange(0, 64, 2, dtype=np.float64) / 64.0)
    ang = pos[:, None] * inv[None]
    cosv = np.cos(ang).astype(np.float32); sinv = np.sin(ang).astype(np.float32)
    c64 = np.repeat(cosv, 2, axis=1).T
    s64 = np.repeat(sinv, 2, axis=1).T
    cos128 = np.concatenate([c64, c64], 0)
    sin128 = np.concatenate([s64, s64], 0)
    toks = slice(TOK * c, TOK * (c + 1))

    pm = np.zeros((64, 64), np.float32)
    for i in range(32):
        pm[2 * i, 2 * i + 1] = -1.0
        pm[2 * i + 1, 2 * i] = 1.0
    pmfull = np.zeros((128, 128), np.float32)
    pmfull[:64, :64] = pm
    pmfull[64:, 64:] = pm
    pmT_ = pmfull.T

    maskp_ = np.zeros((128, 4, 512), np.float32)
    masktp_ = np.zeros((128, 4, 512), np.float32)
    for phs in range(4):
        qloc = phs * 128 + np.arange(128)
        kloc = np.arange(512)
        maskp_[:, phs, :] = np.where(kloc[None, :] <= qloc[:, None], 0.0, -1e9)
        for ks in range(4):
            krow = ks * 128 + np.arange(128)
            qcol = phs * 128 + np.arange(128)
            masktp_[:, phs, ks * 128:(ks + 1) * 128] = np.where(
                krow[:, None] <= qcol[None, :], 0.0, -1e9)

    wq_head_ = np.concatenate([wqu[:, 128 * h0:128 * h0 + 128], wqu[:, 128 * h1:128 * h1 + 128],
                               wqr[:, 64 * h0:64 * h0 + 64], wqr[:, 64 * h1:64 * h1 + 64]], axis=1)
    return dict(
        xT=C(x[toks].T).astype(np.float32),
        w1_s=C(w1[:, 256 * c:256 * (c + 1)].T),
        w2_s=C(w2[:, 256 * c:256 * (c + 1)].T),
        w3_s=C(w3[256 * c:256 * (c + 1), :].T),
        wq_down=wqd, wkv_down=wkvd, wk_rope=wkr, wo=wo,
        wq_head=C(wq_head_),
        wkv_k=C(np.concatenate([wkvu[:, 256 * h0:256 * h0 + 128],
                                wkvu[:, 256 * h1:256 * h1 + 128]], 1)),
        wkv_v=C(np.concatenate([wkvu[:, 256 * h0 + 128:256 * h0 + 256],
                                wkvu[:, 256 * h1 + 128:256 * h1 + 256]], 1)),
        qnw_t=C(qnw_.reshape(QKD, 128).T),
        kvnw_t=C(kvnw_.reshape(KVKD, 128).T),
        onesr=np.ones((1, 128), np.float32),
        idf=np.eye(128, dtype=np.float32),
        idb=np.eye(128, dtype=np.float32).astype(bf),
        pmT=pmT_.astype(bf),
        maskp=C(maskp_.reshape(128, 4 * 512)).astype(bf),
        masktp=C(masktp_.reshape(128, 4 * 512)).astype(bf),
        ropecq=C(cos128 * SCALE).astype(bf),
        ropesq=C(sin128 * SCALE).astype(bf),
        ropeck=C(cos128[:, toks]).astype(bf),
        ropesk=C(sin128[:, toks]).astype(bf),
    )


def kernel(**inputs):
    nc = build_program()
    in_maps = [make_inputs(inputs, c) for c in range(NCORE)]
    res = run_bass_kernel_spmd(nc, in_maps, core_ids=list(range(NCORE)))
    out = np.concatenate([res.results[c]["y"].T for c in range(NCORE)], axis=0)
    return out.reshape(1, S_ALL, DM).astype(np.float32)


# version 02
# speedup vs baseline: 1.1718x; 1.1718x over previous
import contextlib
import numpy as np
import ml_dtypes
import concourse.bass as bass
import concourse.mybir as mybir
from concourse.bass_utils import run_bass_kernel_spmd
from concourse.aot_env import allow_ici_all_to_all

F32 = mybir.dt.float32
BF16 = mybir.dt.bfloat16
AF = mybir.ActivationFunctionType
ALU = mybir.AluOpType
AX = mybir.AxisListType

NCORE = 8
TOK = 1024
S_ALL = 8192
DM = 2048
KD = 16
HID = 5632
HT = 44
QR = 1536
QKD = 12
KVR = 512
KVKD = 4
SCALE = float(1.0 / np.sqrt(192.0))
WCNT = float(HID * DM)
ROWW = 256 * HID
ROWL = (QR + KVR + 128) * TOK
QOFF = QR * TOK
KVOFF = (QR + KVR) * TOK
A2AB = 2 * 128 * TOK

PHASE_LIMIT = 99
DBG = False


class Builder:
    ENGS = ("t", "s", "v", "g", "y")

    def __init__(self):
        self.phase = "P0"
        self.recs = {e: [] for e in self.ENGS}
        self.cnt = {}
        self.sems = {}
        self.last_inc = {e: None for e in self.ENGS}
        self.since_barrier = {e: 0 for e in self.ENGS}

    def sem(self, name):
        self.cnt.setdefault(name, 0)
        return name

    def emit(self, e, fn, inc=None, inc_n=1):
        rec = {"fn": fn, "incs": [], "ph": self.phase}
        if inc is not None:
            self.sem(inc)
            self.cnt[inc] += inc_n
            rec["incs"].append((inc, inc_n))
            self.last_inc[e] = (inc, self.cnt[inc])
        self.recs[e].append(rec)
        self.since_barrier[e] += 1
        return rec

    def dma(self, e, out, in_, inc, **kw):
        assert e in ("y", "g")
        return self.emit(e, lambda eng: eng.dma_start(out=out, in_=in_, **kw),
                         inc=inc, inc_n=16)

    def wait(self, e, sname, cnt=None):
        if cnt is None:
            cnt = self.cnt.get(sname, 0)
        if cnt <= 0:
            return
        self.sem(sname)
        self.recs[e].append(
            {"fn": (lambda s_, c_: lambda eng: eng.wait_ge(self.sems[s_], c_))(sname, cnt),
             "incs": [], "ph": self.phase})
        self.since_barrier[e] += 1

    def mark(self, sname):
        return self.cnt.get(sname, 0)

    def barrier(self):
        marks = {}
        for e in self.ENGS:
            if self.since_barrier[e] == 0:
                continue
            rec = self.recs[e][-1]
            if not rec["incs"]:
                pname = self.sem("prog_" + e)
                self.cnt[pname] += 1
                rec["incs"].append((pname, 1))
                self.last_inc[e] = (pname, self.cnt[pname])
            marks[e] = self.last_inc[e]
        for e in self.ENGS:
            for o, (sname, val) in marks.items():
                if o == e:
                    continue
                self.wait(e, sname, val)
        for e in self.ENGS:
            self.since_barrier[e] = 0

    def replay(self, block, nc=None):
        b = self

        def make(engname):
            def body(eng):
                import contextlib as _cl
                cur = [None]
                stk = _cl.ExitStack()
                for rec in b.recs[engname]:
                    ph = rec.get("ph")
                    if nc is not None and ph != cur[0]:
                        stk.close()
                        stk = _cl.ExitStack()
                        stk.enter_context(nc.named_scope(ph))
                        cur[0] = ph
                    inst = rec["fn"](eng)
                    for (sname, n) in rec["incs"]:
                        inst.then_inc(b.sems[sname], n)
                stk.close()
            return body

        block.tensor(make("t"))
        block.scalar(make("s"))
        block.vector(make("v"))
        block.gpsimd(make("g"))
        block.sync(make("y"))


def build_program():
    nc = bass.Bass("TRN2", target_bir_lowering=False)
    B = Builder()
    E, W, D, M = B.emit, B.wait, B.dma, B.mark

    def ein(name, shape, dt=F32):
        return nc.dram_tensor(name, shape, dt, kind="ExternalInput").ap()

    xT = ein("xT", [DM, TOK])
    w1_s = ein("w1_s", [256, HID]); w2_s = ein("w2_s", [256, HID]); w3_s = ein("w3_s", [HID, 256])
    wq_down = ein("wq_down", [DM, QR]); wkv_down = ein("wkv_down", [DM, KVR])
    wk_rope = ein("wk_rope", [DM, 64]); wo_in = ein("wo", [DM, DM])
    wq_head = ein("wq_head", [QR, 384])
    wkv_k = ein("wkv_k", [KVR, 256]); wkv_v = ein("wkv_v", [KVR, 256])
    qnw_t = ein("qnw_t", [128, QKD]); kvnw_t = ein("kvnw_t", [128, KVKD])
    onesr_in = ein("onesr", [1, 128])
    idf_in = ein("idf", [128, 128]); idb_in = ein("idb", [128, 128], BF16)
    pmT_in = ein("pmT", [128, 128], BF16)
    maskp_in = ein("maskp", [128, 4 * 512], BF16)
    masktp_in = ein("masktp", [128, 4 * 512], BF16)
    ropecq = ein("ropecq", [128, S_ALL], BF16); ropesq = ein("ropesq", [128, S_ALL], BF16)
    ropeck = ein("ropeck", [128, TOK], BF16); ropesk = ein("ropesk", [128, TOK], BF16)

    yout = nc.dram_tensor("y", [DM, TOK], F32, kind="ExternalOutput").ap()

    g_in = nc.dram_tensor("g_in", [1, 4], F32).ap()
    g_out = nc.dram_tensor("g_out", [1, 4], F32).ap()
    agw_in = [nc.dram_tensor(f"agw_in{i}", [1, ROWW], BF16).ap() for i in range(3)]
    agw_out = [nc.dram_tensor(f"agw_out{i}", [8, ROWW], BF16, addr_space="Shared").ap() for i in range(3)]
    agl_in = nc.dram_tensor("agl_in", [1, ROWL], BF16).ap()
    agl_out = nc.dram_tensor("agl_out", [8, ROWL], BF16, addr_space="Shared").ap()
    a2a_in = nc.dram_tensor("a2a_in", [8, A2AB], BF16).ap()
    a2a_out = nc.dram_tensor("a2a_out", [8, A2AB], BF16).ap()

    dbg = {}
    if DBG:
        for nm, shp, dt in [("dbg_resid1", [128, KD * TOK], F32),
                            ("dbg_ln2", [128, KD * TOK], BF16),
                            ("dbg_agl", [8, ROWL], BF16),
                            ("dbg_qn0", [128, S_ALL], BF16),
                            ("dbg_qrp", [128, S_ALL], BF16),
                            ("dbg_kn", [128, S_ALL], BF16),
                            ("dbg_v", [128, 64 * 132], BF16),
                            ("dbg_a2a", [8, A2AB], BF16),
                            ("dbg_resid2", [128, KD * TOK], F32),
                            ("dbg_qw", [8, ROWW], BF16),
                            ("dbg_lnx", [128, KD * TOK], BF16),
                            ("dbg_h", [128, HT * 512], BF16),
                            ("dbg_gthr", [128, 8], F32),
                            ("dbg_redc", [128, 32], F32)]:
            dbg[nm] = nc.dram_tensor(nm, shp, dt, kind="ExternalOutput").ap()

    SBASE = 16512
    STOP = 229344
    off = [0]

    def sat(name, shape, dt, at=None):
        a = off[0] if at is None else at
        th = nc.alloc_sbuf_tensor_at(name, shape, dt, offset=SBASE + a)
        esz = 2 if dt == BF16 else 4
        end = a + int(np.prod(shape[1:])) * esz
        assert SBASE + end <= STOP, f"{name} overflows sbuf: {SBASE+end}"
        if at is None:
            off[0] = (end + 31) // 32 * 32
        return th

    resid = sat("resid", [128, KD * TOK], F32)
    lnbuf = sat("lnbuf", [128, KD * TOK], BF16)
    onesr = sat("onesr_sb", [1, 128], F32)
    onesrow_b = sat("onesrow_b", [1, 256], BF16)
    onescol_f = sat("onescol_f", [128, 8], F32)
    id_f = sat("id_f", [128, 128], F32)
    id_b = sat("id_b", [128, 128], BF16)
    pmT = sat("pmT_sb", [128, 128], BF16)
    onecol_b = sat("onecol_b", [128, 16], BF16)
    qnw = sat("qnw_sb", [128, QKD], F32)
    kvnw = sat("kvnw_sb", [128, KVKD], F32)
    cell_ln = sat("cell_ln", [1, 8], F32)
    cell_rq = sat("cell_rq", [1, 8], F32)
    mcol = sat("mcol", [128, 32], F32)
    mbuf = sat("mbuf", [128, 8], F32)
    rlbuf = sat("rlbuf", [128, 8], F32)
    redc = sat("redc", [128, 32], F32)
    nmrow = sat("nmrow", [1, 256], BF16)
    m_row = sat("m_row", [1, 1024], F32)
    rstd_row = sat("rstd_row", [1, 1024], F32)
    srow = sat("srow", [1, 1024], F32)
    maskp = sat("maskp_sb", [128, 4 * 512], BF16)
    masktp = sat("masktp_sb", [128, 4 * 512], BF16)
    gthr = sat("gthr", [128, 8], F32)
    zcol = sat("zcol", [128, 8], F32)
    tmpf = sat("tmpf", [128, 512], F32)
    sqstage = sat("sqstage", [128, 1024], BF16)
    AREN = off[0]
    AREN_SZ = (STOP - SBASE) - AREN
    assert AREN_SZ >= 78848, f"arena too small: {AREN_SZ}"

    qch = [sat(f"qch{i}", [128, 1408], F32, at=AREN + i * 5632) for i in range(2)]
    qabs = sat("qabs", [128, 1408], F32, at=AREN + 11264)
    qsgn = sat("qsgn", [128, 1408], BF16, at=AREN + 16896)
    qq = [sat(f"qq{i}", [128, 1408], BF16, at=AREN + 19712 + i * 2816) for i in range(2)]
    h_buf = sat("h_buf", [128, HT * 512], BF16, at=AREN)
    qg1 = [sat(f"qg1_{i}", [128, 16 * 256], BF16, at=AREN + 45056 + i * 8192) for i in range(2)]
    qg2 = [sat(f"qg2_{i}", [128, 16 * 256], BF16, at=AREN + 61440 + i * 8192) for i in range(2)]
    q3b = [sat(f"q3b{i}", [128, HID], BF16, at=AREN + 45056 + i * 11264) for i in range(2)]
    sgstage = sat("sgstage", [128, 1024], BF16, at=AREN + 77824)
    qc_raw = sat("qc_raw", [128, QKD * TOK], BF16, at=AREN)
    kvc_raw = sat("kvc_raw", [128, KVKD * TOK], BF16, at=AREN + 24576)
    wdn_buf = [sat(f"wdn{i}", [128, 16 * 512], BF16, at=AREN + 32768 + i * 16384) for i in range(2)]
    wkr_buf = sat("wkr_buf", [128, 16 * 64], BF16, at=AREN + 65536)
    ck_sb = sat("ck_sb", [128, TOK], BF16, at=AREN + 67584)
    sk_sb = sat("sk_sb", [128, TOK], BF16, at=AREN + 69632)
    kr1 = sat("kr1", [128, 1024], BF16, at=AREN + 71680)
    kr2 = sat("kr2", [128, 1024], BF16, at=AREN + 73728)
    q_n = [sat(f"q_n{j}", [128, S_ALL], BF16, at=AREN + j * 16384) for j in range(2)]
    k_nope = sat("k_nope", [128, S_ALL], BF16, at=AREN + 32768)
    v_buf = sat("v_buf", [128, 64 * 132], BF16, at=AREN + 49152)
    kvk_sb = sat("kvk_sb", [128, KVKD * 256], BF16, at=AREN + 66048)
    kvv_sb = sat("kvv_sb", [128, KVKD * 256], BF16, at=AREN + 68096)
    kvwin = [sat(f"kvwin{i}", [128, KVKD * 512], BF16, at=AREN + 70144 + i * 4096) for i in range(2)]
    pT = [sat(f"pT{i}", [128, 512], BF16, at=AREN + 78336 + i * 1024) for i in range(2)]
    o_stage = sat("o_stage", [128, 1024], BF16, at=AREN + 80384)
    qcwin = [sat(f"qcwin{i}", [128, QKD * 512], BF16, at=AREN + 32768 + i * 12288) for i in range(2)]
    whead = sat("whead", [128, QKD * 384], BF16, at=AREN + 57344)
    tabc = [sat(f"tabc{i}", [128, 512], BF16, at=AREN + 66560 + i * 1024) for i in range(2)]
    tabs = [sat(f"tabs{i}", [128, 512], BF16, at=AREN + 68608 + i * 1024) for i in range(2)]
    qrps = sat("qrps", [128, 512], BF16, at=AREN + 70656)
    tabt = sat("tabt", [128, 512], BF16, at=AREN + 71680)
    LOFF = 65536
    krope_all = sat("krope_all", [128, S_ALL], BF16, at=LOFF)
    q_rp = sat("q_rp", [128, S_ALL], BF16, at=LOFF + 16384)
    wo_buf = [sat(f"wo_buf{i}", [128, 16 * 512], BF16, at=AREN + i * 16384) for i in range(2)]
    aow = [sat(f"aow{i}", [128, 16 * 512], BF16, at=AREN + 32768 + i * 16384) for i in range(2)]

    ctx = contextlib.ExitStack()
    pb = [ctx.enter_context(nc.psum_tensor(f"pb{i}", [128, 512], F32)) for i in range(7)]
    pbT = ctx.enter_context(nc.psum_tensor("pbT", [128, 1024], BF16))

    SY, SG, ST_, SS, SV, SC = "sy", "sg", "st", "ss", "sv", "sc"

    D("y", resid[:, :], xT.rearrange("(kd p) t -> p kd t", p=128), SY)
    D("y", onesr[0:1, :], onesr_in, SY)
    D("y", id_f[:, :], idf_in, SY)
    D("y", id_b[:, :], idb_in, SY)
    D("y", pmT[:, :], pmT_in, SY)
    D("y", qnw[:, :], qnw_t, SY)
    D("y", kvnw[:, :], kvnw_t, SY)
    D("y", maskp[:, :], maskp_in, SY)
    D("y", masktp[:, :], masktp_in, SY)
    E("g", lambda g: g.memset(onecol_b[:, :], 1.0))
    E("g", lambda g: g.memset(onesrow_b[:, :], 1.0))
    E("g", lambda g: g.memset(onescol_f[:, :], 1.0))
    E("g", lambda g: g.memset(cell_ln[:, :], 1e-5))
    E("g", lambda g: g.memset(cell_rq[:, :], 1e-6))
    E("g", lambda g: g.memset(zcol[:, :], 0.0), inc=SG)
    B.barrier()

    B.phase = "P1_quant"
    if PHASE_LIMIT >= 1:
        w12v = [w1_s.rearrange("(a p) h -> a p h", p=128),
                w2_s.rearrange("(a p) h -> a p h", p=128)]
        w3v = w3_s.rearrange("(a p) d -> p a d", p=128)
        agwv12 = [agw_in[i].rearrange("o (a p h) -> o a p h", a=2, p=128) for i in range(2)]
        agwv3 = agw_in[2].rearrange("o (a p d) -> o p a d", a=44, p=128)

        def chunk_src(mi, chn):
            if mi < 2:
                return w12v[mi][chn // 4, :, (chn % 4) * 1408:(chn % 4 + 1) * 1408]
            return w3v[:, 4 * chn:4 * chn + 4, :]

        NCH = [8, 8, 11]
        CW = [1408, 1408, 1024]
        ycons = []
        for mi in range(3):
            for chn in range(NCH[mi]):
                bi = chn % 2
                if len(ycons) >= 2:
                    W("y", SV, ycons[-2])
                D("y", qch[bi][:, 0:CW[mi]], chunk_src(mi, chn), f"ya{bi}")
                W("v", f"ya{bi}")
                col = [0, 8, 16][mi] + chn
                E("v", (lambda bi_, col_, w_: lambda v: v.reduce_sum(
                    redc[:, col_:col_ + 1], qch[bi_][:, 0:w_], axis=AX.X,
                    apply_absolute_value=True))(bi, col, CW[mi]), inc=SV)
                ycons.append(M(SV))
        W("v", SV)
        for mi in range(3):
            E("v", (lambda mi_, n_: lambda v: v.reduce_sum(
                redc[:, 28 + mi_:29 + mi_], redc[:, [0, 8, 16][mi_]:[0, 8, 16][mi_] + n_], axis=AX.X))(mi, NCH[mi]),
              inc=SV)
        W("t", SV)
        E("t", lambda t: t.matmul(pb[6][0:1, 0:4], onescol_f[:, 0:1], redc[:, 28:32],
                                  start=True, stop=True), inc=ST_)
        W("v", ST_)
        E("v", lambda v: v.tensor_copy(srow[0:1, 0:4], pb[6][0:1, 0:4]), inc=SV)
        W("y", SV)
        D("y", g_in, srow[0:1, 0:4], SY)
        W("g", SY)
        E("g", lambda g: g.collective_compute(
            "AllReduce", ALU.add, replica_groups=[list(range(NCORE))],
            ins=[g_in.opt()], outs=[g_out.opt()]), inc=SC)
        W("y", SC)
        D("y", srow[0:1, 0:4], g_out, SY)
        W("v", SY)
        E("v", lambda v: v.tensor_scalar(srow[0:1, 4:8], srow[0:1, 0:4], -0.5 / WCNT, -0.5e-8,
                                         op0=ALU.mult, op1=ALU.add), inc=SV)
        E("v", lambda v: v.tensor_scalar(srow[0:1, 0:4], srow[0:1, 0:4], 0.5 / WCNT, 0.5e-8,
                                         op0=ALU.mult, op1=ALU.add), inc=SV)
        W("t", SV)
        E("t", lambda t: t.matmul(pb[6][:, 0:8], onesr[0:1, :], srow[0:1, 0:8],
                                  start=True, stop=True), inc=ST_)
        W("v", ST_)
        E("v", lambda v: v.tensor_copy(gthr[:, 0:8], pb[6][:, 0:8]), inc=SV)
        B.barrier()
        for mi in range(3):
            dmamarks = []
            vq_marks = []
            for chn in range(NCH[mi]):
                bi = chn % 2
                wd = CW[mi]
                if len(vq_marks) >= 2:
                    W("y", SV, vq_marks[-2])
                D("y", qch[bi][:, 0:wd], chunk_src(mi, chn), f"yb{bi}")
                if vq_marks:
                    W("s", SV, vq_marks[-1])
                W("s", f"yb{bi}")
                W("v", f"yb{bi}")
                E("s", (lambda bi_, w_: lambda s: s.sign(qsgn[:, 0:w_], qch[bi_][:, 0:w_]))(bi, wd))
                E("s", (lambda bi_, w_: lambda s: s.activation(
                    qabs[:, 0:w_], qch[bi_][:, 0:w_], AF.Abs))(bi, wd), inc=SS)
                W("s", SS)
                E("s", (lambda w_, mi_: lambda s: s.sign(
                    qabs[:, 0:w_], qabs[:, 0:w_], bias=gthr[:, 4 + mi_:5 + mi_]))(wd, mi), inc=SS)
                if len(dmamarks) >= 2:
                    W("v", SY, dmamarks[-2])
                W("v", SS)
                E("v", (lambda w_: lambda v: v.tensor_relu(qabs[:, 0:w_], qabs[:, 0:w_]))(wd), inc=SV)
                W("v", SV)
                E("v", (lambda bi_, w_: lambda v: v.tensor_mul(
                    qq[bi_][:, 0:w_], qabs[:, 0:w_], qsgn[:, 0:w_]))(bi, wd), inc=SV)
                vq_marks.append(M(SV))
                W("y", SV)
                if mi < 2:
                    dst = agwv12[mi][0, chn // 4, :, (chn % 4) * 1408:(chn % 4 + 1) * 1408]
                    D("y", dst, qq[bi][:, 0:wd], SY)
                else:
                    dst = agwv3[0, :, 4 * chn:4 * chn + 4, :]
                    D("y", dst, qq[bi][:, 0:1024].rearrange("p (e d) -> p e d", e=4), SY)
                dmamarks.append(M(SY))
            W("g", SY)
            E("g", (lambda mi_: lambda g: g.collective_compute(
                "AllGather", ALU.bypass, replica_groups=[list(range(NCORE))],
                ins=[agw_in[mi_].opt()], outs=[agw_out[mi_].opt()]))(mi), inc=SC)
        B.barrier()

    def ln_phase():
        for kd in range(KD):
            W("v", ST_)
            E("v", (lambda kd_: lambda v: v.tensor_copy(
                lnbuf[:, kd_ * 1024:(kd_ + 1) * 1024], resid[:, kd_ * 1024:(kd_ + 1) * 1024]))(kd), inc=SV)
            W("v", SV)
            E("v", (lambda kd_: lambda v: v.tensor_mul(
                sqstage[:, :], lnbuf[:, kd_ * 1024:(kd_ + 1) * 1024],
                lnbuf[:, kd_ * 1024:(kd_ + 1) * 1024]))(kd), inc=SV)
            W("t", SV)
            st0, sp0 = (kd == 0), (kd == KD - 1)
            for half in range(2):
                E("t", (lambda kd_, h_, s0, s1: lambda t: t.matmul(
                    pb[2 + h_][0:1, 0:512], onecol_b[:, 0:1],
                    lnbuf[:, kd_ * 1024 + h_ * 512:kd_ * 1024 + h_ * 512 + 512],
                    start=s0, stop=s1))(kd, half, st0, sp0))
                E("t", (lambda h_, s0, s1: lambda t: t.matmul(
                    pb[4 + h_][0:1, 0:512], onecol_b[:, 0:1],
                    sqstage[:, h_ * 512:h_ * 512 + 512],
                    start=s0, stop=s1))(half, st0, sp0), inc=(ST_ if (half == 1) else None))
        W("v", ST_)
        E("v", lambda v: v.tensor_scalar(m_row[0:1, 0:512], pb[2][0:1, 0:512], 1.0 / DM, None, op0=ALU.mult))
        E("v", lambda v: v.tensor_scalar(m_row[0:1, 512:1024], pb[3][0:1, 0:512], 1.0 / DM, None, op0=ALU.mult))
        E("v", lambda v: v.tensor_scalar(srow[0:1, 0:512], pb[4][0:1, 0:512], 1.0 / DM, None, op0=ALU.mult))
        E("v", lambda v: v.tensor_scalar(srow[0:1, 512:1024], pb[5][0:1, 0:512], 1.0 / DM, None, op0=ALU.mult), inc=SV)
        W("v", SV)
        E("v", lambda v: v.tensor_mul(rstd_row[0:1, :], m_row[0:1, :], m_row[0:1, :]), inc=SV)
        W("v", SV)
        E("v", lambda v: v.tensor_sub(rstd_row[0:1, :], srow[0:1, :], rstd_row[0:1, :]), inc=SV)
        W("s", SV)
        E("s", lambda s: s.activation(rstd_row[0:1, :], rstd_row[0:1, :], AF.Ln, bias=cell_ln[0:1, 0:1]))
        E("s", lambda s: s.activation(rstd_row[0:1, :], rstd_row[0:1, :], AF.Exp, scale=-0.5), inc=SS)
        for half in range(2):
            W("t", SS)
            if half == 1:
                W("t", SV)
            E("t", (lambda h_: lambda t: t.matmul(
                pb[6][:, 0:512], onesr[0:1, :], m_row[0:1, h_ * 512:h_ * 512 + 512],
                start=True, stop=True))(half))
            E("t", (lambda h_: lambda t: t.matmul(
                pb[0][:, 0:512], onesr[0:1, :], rstd_row[0:1, h_ * 512:h_ * 512 + 512],
                start=True, stop=True))(half), inc=ST_)
            W("v", ST_)
            for kd in range(KD):
                c0 = kd * 1024 + half * 512
                E("v", (lambda c_: lambda v: v.tensor_sub(
                    tmpf[:, :], resid[:, c_:c_ + 512], pb[6][:, 0:512]))(c0), inc=SV)
                W("v", SV)
                E("v", (lambda c_: lambda v: v.tensor_mul(
                    lnbuf[:, c_:c_ + 512], tmpf[:, :], pb[0][:, 0:512]))(c0), inc=SV)
        B.barrier()

    def ffn_phase(first):
        agq1 = agw_out[0].rearrange("r (a p h) -> p r a h", a=2, p=128)
        agq2 = agw_out[1].rearrange("r (a p h) -> p r a h", a=2, p=128)
        agq3 = agw_out[2].rearrange("r (a p d) -> r p a d", a=44, p=128)
        if first:
            W("y", SC, 3)
        for half in range(2):
            hoff = half * 512
            sig_marks = []
            hv_marks = []
            for mg in range(22):
                bi = mg % 2
                if mg == 0:
                    W("y", ST_, M(ST_))
                if mg >= 2:
                    W("y", ST_, qg_cons[bi])
                D("y", qg1[bi][:, :].rearrange("p (r a hh) -> p r a hh", r=8, a=2),
                  agq1[:, :, :, mg * 256:(mg + 1) * 256], f"yq{bi}")
                D("y", qg2[bi][:, :].rearrange("p (r a hh) -> p r a hh", r=8, a=2),
                  agq2[:, :, :, mg * 256:(mg + 1) * 256], f"yq{bi}")
                W("t", f"yq{bi}")
                for mt in range(2):
                    ht = mg * 2 + mt
                    pA = pb[ht % 2]
                    pB = pb[2 + ht % 2]
                    if ht >= 2:
                        W("t", SS, sig_marks[ht - 2])
                        W("t", SV, hv_marks[ht - 2])
                    for kdi in range(KD):
                        E("t", (lambda buf_, k_, m_, pA_, h0_: lambda t: t.matmul(
                            pA_[:, 0:512], buf_[:, k_ * 256 + m_ * 128:k_ * 256 + m_ * 128 + 128],
                            lnbuf[:, k_ * 1024 + h0_:k_ * 1024 + h0_ + 512],
                            start=(k_ == 0), stop=(k_ == KD - 1)))(qg1[bi], kdi, mt, pA, hoff),
                          inc=(ST_ if kdi == KD - 1 else None))
                    for kdi in range(KD):
                        E("t", (lambda buf_, k_, m_, pB_, h0_: lambda t: t.matmul(
                            pB_[:, 0:512], buf_[:, k_ * 256 + m_ * 128:k_ * 256 + m_ * 128 + 128],
                            lnbuf[:, k_ * 1024 + h0_:k_ * 1024 + h0_ + 512],
                            start=(k_ == 0), stop=(k_ == KD - 1)))(qg2[bi], kdi, mt, pB, hoff),
                          inc=(ST_ if kdi == KD - 1 else None))
                    W("s", ST_, M(ST_) - 1)
                    if ht >= 2:
                        W("s", SV, hv_marks[ht - 2])
                    E("s", (lambda pA_, ht_: lambda s: s.activation(
                        sgstage[:, (ht_ % 2) * 512:(ht_ % 2) * 512 + 512],
                        pA_[:, 0:512], AF.Sigmoid))(pA, ht), inc=SS)
                    sig_marks.append(M(SS))
                    W("v", SS)
                    W("v", ST_)
                    E("v", (lambda pB_, ht_: lambda v: v.tensor_mul(
                        h_buf[:, ht_ * 512:ht_ * 512 + 512],
                        sgstage[:, (ht_ % 2) * 512:(ht_ % 2) * 512 + 512], pB_[:, 0:512]))(pB, ht),
                      inc=SV)
                    hv_marks.append(M(SV))
                if mg == 0:
                    qg_cons = [None, None]
                qg_cons[bi] = M(ST_)
            h_done = M(SV)
            x12_done = M(ST_)
            q3_cons = [None, None]
            res_marks = []
            for dt in range(KD):
                bi = dt % 2
                if dt == 0:
                    if first and half == 0:
                        W("y", SC, 4)
                    W("y", ST_, x12_done)
                if dt >= 2:
                    W("y", ST_, q3_cons[bi])
                D("y", q3b[bi][:, :].rearrange("p (a dd) -> p a dd", a=44),
                  agq3[dt // 2, :, :, (dt % 2) * 128:(dt % 2) * 128 + 128], f"y3{bi}")
                W("t", f"y3{bi}")
                if dt == 0:
                    W("t", SV, h_done)
                pC = pb[4 + dt % 2]
                if dt >= 2:
                    W("t", SV, res_marks[dt - 2])
                for a in range(HT):
                    E("t", (lambda b_, a_, pC_: lambda t: t.matmul(
                        pC_[:, 0:512], q3b[b_][:, a_ * 128:a_ * 128 + 128],
                        h_buf[:, a_ * 512:a_ * 512 + 512],
                        start=(a_ == 0), stop=(a_ == HT - 1)))(bi, a, pC),
                      inc=(ST_ if a == HT - 1 else None))
                q3_cons[bi] = M(ST_)
                W("v", ST_)
                E("v", (lambda pC_, c_: lambda v: v.scalar_tensor_tensor(
                    resid[:, c_:c_ + 512], pC_[:, 0:512], 0.5, resid[:, c_:c_ + 512],
                    op0=ALU.mult, op1=ALU.add))(pC, dt * 1024 + hoff), inc=SV)
                res_marks.append(M(SV))
        B.barrier()

    if PHASE_LIMIT >= 2:
        B.phase = "P2_ln1"
        ln_phase()
        B.phase = "P2_ffn1"
        if DBG:
            D("y", dbg["dbg_lnx"], lnbuf[:, :], SY)
            B.barrier()
        ffn_phase(first=True)
        if DBG:
            D("y", dbg["dbg_resid1"], resid[:, :], SY)
            W("y", SC)
            D("y", dbg["dbg_qw"], agw_out[0], SY)
            D("y", dbg["dbg_gthr"], gthr[:, :], SY)
            D("y", dbg["dbg_redc"], redc[:, :], SY)
            D("y", dbg["dbg_h"], h_buf[:, :], SY)
    if PHASE_LIMIT >= 3:
        B.phase = "P3_ln2"
        ln_phase()
        if DBG:
            D("y", dbg["dbg_ln2"], lnbuf[:, :], SY)
            B.barrier()

    B.phase = "P4_proj"
    if PHASE_LIMIT >= 4:
        wqdv = wq_down.rearrange("(kd p) m -> p kd m", p=128)
        wkvdv = wkv_down.rearrange("(kd p) m -> p kd m", p=128)
        wkrv = wk_rope.rearrange("(kd p) m -> p kd m", p=128)
        aglq = agl_in.rearrange("o (a p t) -> o a p t", a=QR // 128 + KVR // 128 + 1, p=128)

        def proj_rms(nt, wview, mgroups, statbank, meann, rawbuf):
            wdn_cons = [0, 0]
            W("v", SY)
            for mg in range(mgroups):
                bi = mg % 2
                if mg >= 2:
                    W("g", ST_, wdn_cons[bi])
                D("g", wdn_buf[bi][:, :].rearrange("p (kd mm) -> p kd mm", kd=16),
                  wview[:, :, mg * 512:(mg + 1) * 512], f"gw{bi}")
                W("t", f"gw{bi}")
                for mt in range(4):
                    tile = mg * 4 + mt
                    if tile >= nt:
                        break
                    for half in range(2):
                        pq = pb[(tile * 2 + half) % 2]
                        if tile * 2 + half >= 2:
                            W("t", SS, M(SS) - 1)
                        for kdi in range(KD):
                            E("t", (lambda b_, k_, m_, p_, h_: lambda t: t.matmul(
                                p_[:, 0:512], wdn_buf[b_][:, k_ * 512 + m_ * 128:k_ * 512 + m_ * 128 + 128],
                                lnbuf[:, k_ * 1024 + h_ * 512:k_ * 1024 + h_ * 512 + 512],
                                start=(k_ == 0), stop=(k_ == KD - 1)))(bi, kdi, mt, pq, half),
                              inc=(ST_ if kdi == KD - 1 else None))
                        W("s", ST_)
                        E("s", (lambda p_, c_: lambda s: s.copy(
                            rawbuf[:, c_:c_ + 512], p_[:, 0:512]))(pq, tile * 1024 + half * 512), inc=SS)
                wdn_cons[bi] = M(ST_)
            W("v", SS)
            stat_marks = []
            sv_marks = []
            for tile in range(nt):
                for half in range(2):
                    i = tile * 2 + half
                    if i >= 2:
                        W("v", ST_, stat_marks[i - 2])
                    E("v", (lambda c_, sl_: lambda v: v.tensor_mul(
                        sqstage[:, sl_ * 512:sl_ * 512 + 512], rawbuf[:, c_:c_ + 512],
                        rawbuf[:, c_:c_ + 512]))(tile * 1024 + half * 512, i % 2), inc=SV)
                    sv_marks.append(M(SV))
                    W("t", SV, sv_marks[-1])
                    E("t", (lambda h_, s0, s1, sb_, sl_: lambda t: t.matmul(
                        pb[sb_ + h_][0:1, 0:512], onecol_b[:, 0:1],
                        sqstage[:, sl_ * 512:sl_ * 512 + 512],
                        start=s0, stop=s1))(half, tile == 0, tile == nt - 1, statbank, i % 2),
                      inc=ST_)
                    stat_marks.append(M(ST_))
            W("s", ST_)
            E("s", (lambda sb_, mn_: lambda s: s.activation(
                rstd_row[0:1, 0:512], pb[sb_][0:1, 0:512], AF.Ln, bias=cell_rq[0:1, 0:1],
                scale=1.0 / mn_))(statbank, meann))
            E("s", (lambda sb_, mn_: lambda s: s.activation(
                rstd_row[0:1, 512:1024], pb[sb_ + 1][0:1, 0:512], AF.Ln, bias=cell_rq[0:1, 0:1],
                scale=1.0 / mn_))(statbank, meann))
            E("s", lambda s: s.activation(rstd_row[0:1, :], rstd_row[0:1, :], AF.Exp, scale=-0.5), inc=SS)
            for half in range(2):
                W("t", SS)
                if half == 1:
                    W("t", SY)
                E("t", (lambda h_: lambda t: t.matmul(
                    pb[6][:, 0:512], onesr[0:1, :], rstd_row[0:1, h_ * 512:h_ * 512 + 512],
                    start=True, stop=True))(half), inc=ST_)
                W("v", ST_)
                for tile in range(nt):
                    stg = sqstage[:, (tile % 2) * 512:(tile % 2) * 512 + 512]
                    if tile >= 2:
                        W("v", SY, M(SY) - 16)
                    E("v", (lambda stg_, c_: lambda v: v.tensor_mul(
                        stg_, rawbuf[:, c_:c_ + 512], pb[6][:, 0:512]))(stg, tile * 1024 + half * 512),
                      inc=SV)
                    W("y", SV)
                    D("y", (lambda t_, h_: aglq[0, t_ + (0 if rawbuf is qc_raw else QR // 128), :,
                                                h_ * 512:h_ * 512 + 512])(tile, half), stg, SY)

        proj_rms(QKD, wqdv, 3, 2, QR, qc_raw)
        B.barrier()
        proj_rms(KVKD, wkvdv, 1, 2, KVR, kvc_raw)
        B.barrier()
        D("y", ck_sb[:, :], ropeck, SY)
        D("y", sk_sb[:, :], ropesk, SY)
        D("g", wkr_buf[:, :].rearrange("p (kd mm) -> p kd mm", kd=16), wkrv, SG)
        W("t", SG)
        for half in range(2):
            for base in (0, 64):
                for kdi in range(KD):
                    E("t", (lambda k_, b_, h_: lambda t: t.matmul(
                        pb[1][b_:b_ + 64, 0:512], wkr_buf[:, k_ * 64:k_ * 64 + 64],
                        lnbuf[:, k_ * 1024 + h_ * 512:k_ * 1024 + h_ * 512 + 512],
                        start=(k_ == 0), stop=(k_ == KD - 1)))(kdi, base, half),
                      inc=(ST_ if kdi == KD - 1 else None))
            W("s", ST_)
            E("s", (lambda h_: lambda s: s.copy(kr1[:, h_ * 512:h_ * 512 + 512], pb[1][:, 0:512]))(half),
              inc=SS)
            W("t", SS)
            E("t", (lambda h_: lambda t: t.matmul(
                pb[0][:, 0:512], pmT[:, :], kr1[:, h_ * 512:h_ * 512 + 512],
                start=True, stop=True))(half), inc=ST_)
            W("v", SS)
            E("v", (lambda h_: lambda v: v.tensor_mul(
                kr2[:, h_ * 512:h_ * 512 + 512], kr1[:, h_ * 512:h_ * 512 + 512],
                ck_sb[:, h_ * 512:h_ * 512 + 512]))(half))
            W("v", ST_)
            E("v", (lambda h_: lambda v: v.tensor_mul(
                kr1[:, h_ * 512:h_ * 512 + 512], pb[0][:, 0:512],
                sk_sb[:, h_ * 512:h_ * 512 + 512]))(half), inc=SV)
            W("v", SV)
            E("v", (lambda h_: lambda v: v.tensor_add(
                kr2[:, h_ * 512:h_ * 512 + 512], kr2[:, h_ * 512:h_ * 512 + 512],
                kr1[:, h_ * 512:h_ * 512 + 512]))(half), inc=SV)
        W("y", SV)
        D("y", aglq[0, QR // 128 + KVR // 128, :, :], kr2[:, :], SY)
        W("g", SY)
        E("g", lambda g: g.collective_compute(
            "AllGather", ALU.bypass, replica_groups=[list(range(NCORE))],
            ins=[agl_in.opt()], outs=[agl_out.opt()]), inc=SC)
        B.barrier()
        if DBG:
            W("y", SC)
            D("y", dbg["dbg_agl"], agl_out, SY)
            B.barrier()

    B.phase = "P5a_qbuild"
    if PHASE_LIMIT >= 5:
        W("y", SC)
        aglqv = agl_out[:, 0:QOFF].rearrange("r (a p t) -> r p a t", p=128, t=1024)
        aglkv = agl_out[:, QOFF:KVOFF].rearrange("r (a p t) -> r p a t", p=128, t=1024)
        aglkr = agl_out[:, KVOFF:ROWL].rearrange("r (p t) -> p r t", p=128)
        D("g", whead[:, :].rearrange("p (kd mm) -> p kd mm", kd=QKD),
          wq_head.rearrange("(kd p) m -> p kd m", p=128), SG)
        W("v", SG)
        for kdi in range(QKD):
            E("v", (lambda k_: lambda v: v.tensor_scalar(
                whead[:, k_ * 384:k_ * 384 + 384], whead[:, k_ * 384:k_ * 384 + 384],
                qnw[:, k_:k_ + 1], None, op0=ALU.mult))(kdi), inc=(SV if kdi == QKD - 1 else None))
        D("y", krope_all[:, :].rearrange("p (r t) -> p r t", r=8), aglkr, SY)
        W("t", SV)
        qc_cons = [0, 0]
        rp_cons = [0, 0]
        for w in range(16):
            bi = w % 2
            if w >= 2:
                W("y", ST_, qc_cons[bi])
                W("y", SV, rp_cons[bi])
            D("y", qcwin[bi][:, :].rearrange("p (a tt) -> p a tt", a=QKD),
              aglqv[w // 2, :, :, (w % 2) * 512:(w % 2) * 512 + 512], f"yc{bi}")
            D("y", tabc[bi][:, :], ropecq[:, w * 512:(w + 1) * 512], f"yc{bi}")
            D("y", tabs[bi][:, :], ropesq[:, w * 512:(w + 1) * 512], f"yc{bi}")
            W("t", f"yc{bi}")
            W("v", f"yc{bi}")
            for (mi, pdst) in ((0, pb[0]), (1, pb[1]), (2, pb[2])):
                for kdi in range(QKD):
                    E("t", (lambda b_, k_, m_, p_: lambda t: t.matmul(
                        p_[:, 0:512], whead[:, k_ * 384 + m_ * 128:k_ * 384 + m_ * 128 + 128],
                        qcwin[b_][:, k_ * 512:k_ * 512 + 512],
                        start=(k_ == 0), stop=(k_ == QKD - 1)))(bi, kdi, mi, pdst),
                      inc=(ST_ if kdi == QKD - 1 else None))
            W("s", ST_)
            E("s", (lambda w_: lambda s: s.activation(
                q_n[0][:, w_ * 512:w_ * 512 + 512], pb[0][:, 0:512], AF.Copy, scale=SCALE))(w))
            E("s", (lambda w_: lambda s: s.activation(
                q_n[1][:, w_ * 512:w_ * 512 + 512], pb[1][:, 0:512], AF.Copy, scale=SCALE))(w))
            E("s", lambda s: s.copy(qrps[:, :], pb[2][:, 0:512]), inc=SS)
            W("t", SS)
            E("t", lambda t: t.matmul(pb[3][:, 0:512], pmT[:, :], qrps[:, :],
                                      start=True, stop=True), inc=ST_)
            W("v", SS)
            E("v", (lambda b_: lambda v: v.tensor_mul(tabt[:, :], qrps[:, :], tabc[b_][:, :]))(bi))
            W("v", ST_)
            E("v", (lambda b_: lambda v: v.tensor_mul(qrps[:, :], pb[3][:, 0:512], tabs[b_][:, :]))(bi), inc=SV)
            W("v", SV)
            E("v", (lambda w_: lambda v: v.tensor_add(
                q_rp[:, w_ * 512:w_ * 512 + 512], tabt[:, :], qrps[:, :]))(w), inc=SV)
            qc_cons[bi] = M(ST_)
            rp_cons[bi] = M(SV)
        B.barrier()
        if DBG:
            D("y", dbg["dbg_qn0"], q_n[0][:, :], SY)
            D("y", dbg["dbg_qrp"], q_rp[:, :], SY)
            B.barrier()

        D("g", kvk_sb[:, :].rearrange("p (kd mm) -> p kd mm", kd=KVKD),
          wkv_k.rearrange("(kd p) m -> p kd m", p=128), SG)
        D("g", kvv_sb[:, :].rearrange("p (kd mm) -> p kd mm", kd=KVKD),
          wkv_v.rearrange("(kd p) m -> p kd m", p=128), SG)
        W("v", SG)
        for kdi in range(KVKD):
            E("v", (lambda k_: lambda v: v.tensor_scalar(
                kvk_sb[:, k_ * 256:k_ * 256 + 256], kvk_sb[:, k_ * 256:k_ * 256 + 256],
                kvnw[:, k_:k_ + 1], None, op0=ALU.mult))(kdi))
            E("v", (lambda k_: lambda v: v.tensor_scalar(
                kvv_sb[:, k_ * 256:k_ * 256 + 256], kvv_sb[:, k_ * 256:k_ * 256 + 256],
                kvnw[:, k_:k_ + 1], None, op0=ALU.mult))(kdi), inc=(SV if kdi == KVKD - 1 else None))
        B.barrier()

        o_dma_mark = [None]
        for j in range(2):
            B.phase = f"P5b_kv{j}"
            if j == 0:
                E("g", lambda g: g.memset(v_buf[:, :], 1.0), inc=SG)
                W("s", SG)
            kv_cons = [0, 0]
            for w in range(16):
                bi = w % 2
                if w >= 2:
                    W("y", ST_, kv_cons[bi])
                D("y", kvwin[bi][:, :].rearrange("p (a tt) -> p a tt", a=KVKD),
                  aglkv[w // 2, :, :, (w % 2) * 512:(w % 2) * 512 + 512], f"yk{bi}")
                W("t", f"yk{bi}")
                for kdi in range(KVKD):
                    E("t", (lambda b_, k_, j_: lambda t: t.matmul(
                        pb[0][:, 0:512], kvk_sb[:, k_ * 256 + j_ * 128:k_ * 256 + j_ * 128 + 128],
                        kvwin[b_][:, k_ * 512:k_ * 512 + 512],
                        start=(k_ == 0), stop=(k_ == KVKD - 1)))(bi, kdi, j),
                      inc=(ST_ if kdi == KVKD - 1 else None))
                W("s", ST_)
                E("s", (lambda w_: lambda s: s.copy(
                    k_nope[:, w_ * 512:w_ * 512 + 512], pb[0][:, 0:512]))(w), inc=SS)
                for tt in range(4):
                    pv_ = pb[2 + tt % 2]
                    if w > 0 or tt >= 2:
                        W("t", SS, M(SS) - 1)
                    for kdi in range(KVKD):
                        E("t", (lambda b_, k_, t2, j_, p_: lambda t: t.matmul(
                            p_[:, 0:128], kvwin[b_][:, k_ * 512 + t2 * 128:k_ * 512 + t2 * 128 + 128],
                            kvv_sb[:, k_ * 256 + j_ * 128:k_ * 256 + j_ * 128 + 128],
                            start=(k_ == 0), stop=(k_ == KVKD - 1)))(bi, kdi, tt, j, pv_),
                          inc=(ST_ if kdi == KVKD - 1 else None))
                    W("s", ST_)
                    E("s", (lambda p_, c_: lambda s: s.copy(
                        v_buf[:, c_:c_ + 128], p_[:, 0:128]))(pv_, (4 * w + tt) * 132), inc=SS)
                kv_cons[bi] = M(ST_)
            B.barrier()
            if DBG and j == 0:
                D("y", dbg["dbg_kn"], k_nope[:, :], SY)
                D("y", dbg["dbg_v"], v_buf[:, :], SY)
                B.barrier()

            B.phase = f"P5c_attn{j}"
            s_red_marks = []
            nm_marks = {}
            exp_marks = {}
            pv_marks = {}
            st_marks = {}
            exp_by_par = [0, 0]
            pv_by_par = [0, 0]

            def emit_pass1(qt):
                R = qt // 4 + 1
                ph = qt % 4
                sl = (qt % 2) * 16
                for kc in range(R):
                    sb_ = pb[kc % 2]
                    if len(s_red_marks) >= 2:
                        W("t", SV, s_red_marks[-2])
                    E("t", (lambda j_, q_, k_, p_: lambda t: t.matmul(
                        p_[:, 0:512], q_n[j_][:, q_ * 128:q_ * 128 + 128],
                        k_nope[:, k_ * 512:k_ * 512 + 512], start=True, stop=False))(j, qt, kc, sb_))
                    E("t", (lambda j_, q_, k_, p_: lambda t: t.matmul(
                        p_[:, 0:512], q_rp[64 * j_:64 * j_ + 64, q_ * 128:q_ * 128 + 128],
                        krope_all[64 * j_:64 * j_ + 64, k_ * 512:k_ * 512 + 512],
                        start=False, stop=True))(j, qt, kc, sb_), inc=ST_)
                    W("v", ST_)
                    if kc == R - 1:
                        E("v", (lambda p_, ph_: lambda v: v.tensor_add(
                            p_[:, 0:512], p_[:, 0:512], maskp[:, ph_ * 512:ph_ * 512 + 512]))(sb_, ph),
                          inc=SV)
                        W("v", SV)
                    E("v", (lambda p_, k_, sl_: lambda v: v.reduce_max(
                        mcol[:, sl_ + k_:sl_ + k_ + 1], p_[:, 0:512], axis=AX.X))(sb_, kc, sl), inc=SV)
                    s_red_marks.append(M(SV))
                W("v", SV)
                E("v", (lambda r_, sl_, q2: lambda v: v.reduce_max(
                    mbuf[:, q2:q2 + 1], mcol[:, sl_:sl_ + r_], axis=AX.X))(R, sl, qt % 2), inc=SV)
                W("t", SV)
                if qt >= 1:
                    W("t", SS, nm_marks[qt - 1])
                E("t", (lambda q2: lambda t: t.transpose(
                    pb[6][0:1, 0:128], mbuf[:, q2:q2 + 1], id_f[:, :]))(qt % 2), inc=ST_)
                W("s", ST_)
                E("s", (lambda q2: lambda s: s.activation(
                    nmrow[0:1, q2 * 128:q2 * 128 + 128], pb[6][0:1, 0:128], AF.Copy,
                    scale=-1.0))(qt % 2), inc=SS)
                nm_marks[qt] = M(SS)

            def emit_st(qt, kc):
                R = qt // 4 + 1
                stb = pb[2 + kc % 2]
                if exp_by_par[kc % 2]:
                    W("t", SS, exp_by_par[kc % 2])
                for ks in range(4):
                    E("t", (lambda j_, q_, k_, ks_, p_: lambda t: t.matmul(
                        p_[:, ks_ * 128:ks_ * 128 + 128],
                        k_nope[:, k_ * 512 + ks_ * 128:k_ * 512 + ks_ * 128 + 128],
                        q_n[j_][:, q_ * 128:q_ * 128 + 128], start=True, stop=False))(j, qt, kc, ks, stb))
                    E("t", (lambda j_, q_, k_, ks_, p_: lambda t: t.matmul(
                        p_[:, ks_ * 128:ks_ * 128 + 128],
                        krope_all[64 * j_:64 * j_ + 64, k_ * 512 + ks_ * 128:k_ * 512 + ks_ * 128 + 128],
                        q_rp[64 * j_:64 * j_ + 64, q_ * 128:q_ * 128 + 128],
                        start=False, stop=False))(j, qt, kc, ks, stb))
                    E("t", (lambda ks_, p_, q2: lambda t: t.matmul(
                        p_[:, ks_ * 128:ks_ * 128 + 128], onesrow_b[0:1, 0:128],
                        nmrow[0:1, q2 * 128:q2 * 128 + 128], start=False, stop=True))(ks, stb, qt % 2),
                      inc=(ST_ if ks == 3 else None))
                st_marks[(qt, kc)] = M(ST_)

            def emit_exp_pv(qt, kc):
                R = qt // 4 + 1
                ph = qt % 4
                stb = pb[2 + kc % 2]
                Ob = pb[4 + qt % 2]
                if kc == R - 1:
                    W("v", ST_, st_marks[(qt, kc)])
                    E("v", (lambda p_, ph_: lambda v: v.tensor_add(
                        p_[:, 0:512], p_[:, 0:512], masktp[:, ph_ * 512:ph_ * 512 + 512]))(stb, ph),
                      inc=SV)
                    W("s", SV)
                else:
                    W("s", ST_, st_marks[(qt, kc)])
                if pv_by_par[kc % 2]:
                    W("s", ST_, pv_by_par[kc % 2])
                E("s", (lambda p_, b_: lambda s: s.activation(
                    pT[b_][:, :], p_[:, 0:512], AF.Exp))(stb, kc % 2), inc=SS)
                exp_marks[(qt, kc)] = M(SS)
                exp_by_par[kc % 2] = M(SS)
                W("t", SS, exp_marks[(qt, kc)])
                for ks in range(4):
                    E("t", (lambda b_, k_, ks_, O_, s0, s1: lambda t: t.matmul(
                        O_[:, 0:132], pT[b_][:, ks_ * 128:ks_ * 128 + 128],
                        v_buf[:, (k_ * 4 + ks_) * 132:(k_ * 4 + ks_) * 132 + 132],
                        start=s0, stop=s1))(kc % 2, kc, ks, Ob,
                                            (kc == 0 and ks == 0), (kc == R - 1 and ks == 3)),
                      inc=(ST_ if ks == 3 else None))
                pv_marks[(qt, kc)] = M(ST_)
                pv_by_par[kc % 2] = M(ST_)

            def emit_pass2(qt):
                R = qt // 4 + 1
                Ob = pb[4 + qt % 2]
                W("t", SS, nm_marks[qt])
                emit_st(qt, 0)
                for kc in range(R):
                    if kc + 1 < R:
                        emit_st(qt, kc + 1)
                    emit_exp_pv(qt, kc)
                W("v", ST_, pv_marks[(qt, R - 1)])
                E("v", (lambda O_: lambda v: v.reciprocal(rlbuf[:, 0:1], O_[:, 128:129]))(Ob), inc=SV)
                W("s", SV)
                if pv_by_par[0]:
                    W("s", ST_, pv_by_par[0])
                E("s", (lambda O_: lambda s: s.activation(
                    pT[0][:, 0:128], O_[:, 0:128], AF.Copy, scale=rlbuf[:, 0:1]))(Ob), inc=SS)
                W("t", SS)
                E("t", lambda t: t.transpose(pbT[:, 0:128], pT[0][:, 0:128], id_b[:, :]), inc=ST_)
                pv_by_par[0] = max(pv_by_par[0], M(ST_))
                W("s", ST_)
                if qt % 8 == 0 and o_dma_mark[0] is not None:
                    W("s", SY, o_dma_mark[0])
                E("s", (lambda q_: lambda s: s.copy(
                    o_stage[:, (q_ % 8) * 128:(q_ % 8) * 128 + 128], pbT[:, 0:128]))(qt), inc=SS)
                if qt % 8 == 7:
                    W("y", SS)
                    D("y", a2a_in[qt // 8:qt // 8 + 1,
                                  j * 131072:(j + 1) * 131072].rearrange("o (p t) -> o p t", p=128)[0],
                      o_stage[:, :], SY)
                    o_dma_mark[0] = M(SY)

            for qt in range(65):
                if qt < 64:
                    emit_pass1(qt)
                if qt >= 1:
                    emit_pass2(qt - 1)
            B.barrier()
        B.phase = "P5d_a2a"
        W("g", SY)
        E("g", lambda g: g.collective_compute(
            "AllToAll", ALU.bypass, replica_groups=[list(range(NCORE))],
            ins=[a2a_in.opt()], outs=[a2a_out.opt()]), inc=SC)
        B.barrier()
        if DBG:
            W("y", SC)
            D("y", dbg["dbg_a2a"], a2a_out, SY)
            B.barrier()

    B.phase = "P6_wo"
    if PHASE_LIMIT >= 6:
        W("y", SC)
        a2av = a2a_out.rearrange("r (jj p t) -> p r jj t", jj=2, p=128)
        wov = wo_in.rearrange("(kv p) m -> p kv m", p=128)
        for half in range(2):
            D("y", aow[half][:, :].rearrange("p (r jj tt) -> p r jj tt", r=8, jj=2),
              a2av[:, :, :, half * 512:half * 512 + 512], SY)
        for mg in range(4):
            bi = mg % 2
            if mg >= 2:
                W("g", ST_, M(ST_))
            D("g", wo_buf[bi][:, :].rearrange("p (kv mm) -> p kv mm", kv=16),
              wov[:, :, mg * 512:(mg + 1) * 512], f"go{bi}")
            W("t", f"go{bi}")
            W("t", SY)
            for mt in range(4):
                dt = mg * 4 + mt
                for half in range(2):
                    pw = pb[(dt * 2 + half) % 2]
                    if dt * 2 + half >= 2:
                        W("t", SV, M(SV) - 1)
                    for kv in range(KD):
                        E("t", (lambda b_, k_, m_, h_, p_: lambda t: t.matmul(
                            p_[:, 0:512], wo_buf[b_][:, k_ * 512 + m_ * 128:k_ * 512 + m_ * 128 + 128],
                            aow[h_][:, k_ * 512:k_ * 512 + 512],
                            start=(k_ == 0), stop=(k_ == KD - 1)))(bi, kv, mt, half, pw),
                          inc=(ST_ if kv == KD - 1 else None))
                    W("v", ST_)
                    E("v", (lambda p_, c_: lambda v: v.scalar_tensor_tensor(
                        resid[:, c_:c_ + 512], p_[:, 0:512], 1.0, resid[:, c_:c_ + 512],
                        op0=ALU.mult, op1=ALU.add))(pw, dt * 1024 + half * 512), inc=SV)
        B.barrier()
        if DBG:
            D("y", dbg["dbg_resid2"], resid[:, :], SY)
            B.barrier()

    if PHASE_LIMIT >= 7:
        B.phase = "P7_ln3"
        ln_phase()
        B.phase = "P7_ffn2"
        ffn_phase(first=False)

    B.phase = "P8_out"
    D("y", yout.rearrange("(kd p) t -> p kd t", p=128), resid[:, :], SY)
    B.barrier()
    W("y", SY)

    with ctx:
        with nc.Block() as block:
            with contextlib.ExitStack() as semctx:
                for sname in B.cnt:
                    B.sems[sname] = semctx.enter_context(nc.semaphore(sname))
                with allow_ici_all_to_all():
                    B.replay(block, nc)
    return nc


def make_inputs(full, core):
    c = core
    h0, h1 = 2 * c, 2 * c + 1
    x = np.asarray(full["L0_attn_x"])[0]
    w1 = np.asarray(full["L0_ffn_w1"]); w2 = np.asarray(full["L0_ffn_w2"]); w3 = np.asarray(full["L0_ffn_w3"])
    wqd = np.asarray(full["L0_attn_wq_down"]); qnw_ = np.asarray(full["L0_attn_q_norm_w"])
    wqu = np.asarray(full["L0_attn_wq_up"]); wqr = np.asarray(full["L0_attn_wq_rope"])
    wkvd = np.asarray(full["L0_attn_wkv_down"]); kvnw_ = np.asarray(full["L0_attn_kv_norm_w"])
    wkvu = np.asarray(full["L0_attn_wkv_up"]); wkr = np.asarray(full["L0_attn_wk_rope"])
    wo = np.asarray(full["L0_attn_wo"])
    bf = ml_dtypes.bfloat16
    C = np.ascontiguousarray

    pos = np.arange(S_ALL, dtype=np.float64)
    inv = 1.0 / 10000.0 ** (np.arange(0, 64, 2, dtype=np.float64) / 64.0)
    ang = pos[:, None] * inv[None]
    cosv = np.cos(ang).astype(np.float32); sinv = np.sin(ang).astype(np.float32)
    c64 = np.repeat(cosv, 2, axis=1).T
    s64 = np.repeat(sinv, 2, axis=1).T
    cos128 = np.concatenate([c64, c64], 0)
    sin128 = np.concatenate([s64, s64], 0)
    toks = slice(TOK * c, TOK * (c + 1))

    pm = np.zeros((64, 64), np.float32)
    for i in range(32):
        pm[2 * i, 2 * i + 1] = -1.0
        pm[2 * i + 1, 2 * i] = 1.0
    pmfull = np.zeros((128, 128), np.float32)
    pmfull[:64, :64] = pm
    pmfull[64:, 64:] = pm
    pmT_ = pmfull.T

    maskp_ = np.zeros((128, 4, 512), np.float32)
    masktp_ = np.zeros((128, 4, 512), np.float32)
    for phs in range(4):
        qloc = phs * 128 + np.arange(128)
        kloc = np.arange(512)
        maskp_[:, phs, :] = np.where(kloc[None, :] <= qloc[:, None], 0.0, -1e9)
        for ks in range(4):
            krow = ks * 128 + np.arange(128)
            qcol = phs * 128 + np.arange(128)
            masktp_[:, phs, ks * 128:(ks + 1) * 128] = np.where(
                krow[:, None] <= qcol[None, :], 0.0, -1e9)

    wq_head_ = np.concatenate([wqu[:, 128 * h0:128 * h0 + 128], wqu[:, 128 * h1:128 * h1 + 128],
                               wqr[:, 64 * h0:64 * h0 + 64], wqr[:, 64 * h1:64 * h1 + 64]], axis=1)
    return dict(
        xT=C(x[toks].T).astype(np.float32),
        w1_s=C(w1[:, 256 * c:256 * (c + 1)].T),
        w2_s=C(w2[:, 256 * c:256 * (c + 1)].T),
        w3_s=C(w3[256 * c:256 * (c + 1), :].T),
        wq_down=wqd, wkv_down=wkvd, wk_rope=wkr, wo=wo,
        wq_head=C(wq_head_),
        wkv_k=C(np.concatenate([wkvu[:, 256 * h0:256 * h0 + 128],
                                wkvu[:, 256 * h1:256 * h1 + 128]], 1)),
        wkv_v=C(np.concatenate([wkvu[:, 256 * h0 + 128:256 * h0 + 256],
                                wkvu[:, 256 * h1 + 128:256 * h1 + 256]], 1)),
        qnw_t=C(qnw_.reshape(QKD, 128).T),
        kvnw_t=C(kvnw_.reshape(KVKD, 128).T),
        onesr=np.ones((1, 128), np.float32),
        idf=np.eye(128, dtype=np.float32),
        idb=np.eye(128, dtype=np.float32).astype(bf),
        pmT=pmT_.astype(bf),
        maskp=C(maskp_.reshape(128, 4 * 512)).astype(bf),
        masktp=C(masktp_.reshape(128, 4 * 512)).astype(bf),
        ropecq=C(cos128 * SCALE).astype(bf),
        ropesq=C(sin128 * SCALE).astype(bf),
        ropeck=C(cos128[:, toks]).astype(bf),
        ropesk=C(sin128[:, toks]).astype(bf),
    )


def kernel(**inputs):
    nc = build_program()
    in_maps = [make_inputs(inputs, c) for c in range(NCORE)]
    res = run_bass_kernel_spmd(nc, in_maps, core_ids=list(range(NCORE)))
    out = np.concatenate([res.results[c]["y"].T for c in range(NCORE)], axis=0)
    return out.reshape(1, S_ALL, DM).astype(np.float32)


# version 04 (final)
# speedup vs baseline: 1.2005x; 1.0244x over previous
import contextlib
import numpy as np
import ml_dtypes
import concourse.bass as bass
import concourse.mybir as mybir
from concourse.bass_utils import run_bass_kernel_spmd
from concourse.aot_env import allow_ici_all_to_all

F32 = mybir.dt.float32
BF16 = mybir.dt.bfloat16
AF = mybir.ActivationFunctionType
ALU = mybir.AluOpType
AX = mybir.AxisListType

NCORE = 8
TOK = 1024
S_ALL = 8192
DM = 2048
KD = 16
HID = 5632
HT = 44
QR = 1536
QKD = 12
KVR = 512
KVKD = 4
SCALE = float(1.0 / np.sqrt(192.0))
WCNT = float(HID * DM)
ROWW = 256 * HID
ROWL = (QR + KVR + 128) * TOK
QOFF = QR * TOK
KVOFF = (QR + KVR) * TOK
A2AB = 2 * 128 * TOK

PHASE_LIMIT = 99
DBG = False


class Builder:
    ENGS = ("t", "s", "v", "g", "y")

    def __init__(self):
        self.phase = "P0"
        self.recs = {e: [] for e in self.ENGS}
        self.cnt = {}
        self.sems = {}
        self.last_inc = {e: None for e in self.ENGS}
        self.since_barrier = {e: 0 for e in self.ENGS}

    def sem(self, name):
        self.cnt.setdefault(name, 0)
        return name

    def emit(self, e, fn, inc=None, inc_n=1):
        rec = {"fn": fn, "incs": [], "ph": self.phase}
        if inc is not None:
            self.sem(inc)
            self.cnt[inc] += inc_n
            rec["incs"].append((inc, inc_n))
            self.last_inc[e] = (inc, self.cnt[inc])
        self.recs[e].append(rec)
        self.since_barrier[e] += 1
        return rec

    def dma(self, e, out, in_, inc, **kw):
        assert e in ("y", "g")
        return self.emit(e, lambda eng: eng.dma_start(out=out, in_=in_, **kw),
                         inc=inc, inc_n=16)

    def wait(self, e, sname, cnt=None):
        if cnt is None:
            cnt = self.cnt.get(sname, 0)
        if cnt <= 0:
            return
        self.sem(sname)
        self.recs[e].append(
            {"fn": (lambda s_, c_: lambda eng: eng.wait_ge(self.sems[s_], c_))(sname, cnt),
             "incs": [], "ph": self.phase})
        self.since_barrier[e] += 1

    def mark(self, sname):
        return self.cnt.get(sname, 0)

    def barrier(self):
        marks = {}
        for e in self.ENGS:
            if self.since_barrier[e] == 0:
                continue
            rec = self.recs[e][-1]
            if not rec["incs"]:
                pname = self.sem("prog_" + e)
                self.cnt[pname] += 1
                rec["incs"].append((pname, 1))
                self.last_inc[e] = (pname, self.cnt[pname])
            marks[e] = self.last_inc[e]
        for e in self.ENGS:
            for o, (sname, val) in marks.items():
                if o == e:
                    continue
                self.wait(e, sname, val)
        for e in self.ENGS:
            self.since_barrier[e] = 0

    def replay(self, block, nc=None):
        b = self

        def make(engname):
            def body(eng):
                import contextlib as _cl
                cur = [None]
                stk = _cl.ExitStack()
                for rec in b.recs[engname]:
                    ph = rec.get("ph")
                    if nc is not None and ph != cur[0]:
                        stk.close()
                        stk = _cl.ExitStack()
                        stk.enter_context(nc.named_scope(ph))
                        cur[0] = ph
                    inst = rec["fn"](eng)
                    for (sname, n) in rec["incs"]:
                        inst.then_inc(b.sems[sname], n)
                stk.close()
            return body

        block.tensor(make("t"))
        block.scalar(make("s"))
        block.vector(make("v"))
        block.gpsimd(make("g"))
        block.sync(make("y"))


def build_program():
    nc = bass.Bass("TRN2", target_bir_lowering=False)
    B = Builder()
    E, W, D, M = B.emit, B.wait, B.dma, B.mark

    def ein(name, shape, dt=F32):
        return nc.dram_tensor(name, shape, dt, kind="ExternalInput").ap()

    xT = ein("xT", [DM, TOK])
    w1_s = ein("w1_s", [256, HID]); w2_s = ein("w2_s", [256, HID]); w3_s = ein("w3_s", [HID, 256])
    wq_down = ein("wq_down", [DM, QR]); wkv_down = ein("wkv_down", [DM, KVR])
    wk_rope = ein("wk_rope", [DM, 64]); wo_in = ein("wo", [DM, DM])
    wq_head = ein("wq_head", [QR, 384])
    wkv_k = ein("wkv_k", [KVR, 256]); wkv_v = ein("wkv_v", [KVR, 256])
    qnw_t = ein("qnw_t", [128, QKD]); kvnw_t = ein("kvnw_t", [128, KVKD])
    onesr_in = ein("onesr", [1, 128])
    idf_in = ein("idf", [128, 128]); idb_in = ein("idb", [128, 128], BF16)
    pmT_in = ein("pmT", [128, 128], BF16)
    maskp_in = ein("maskp", [128, 4 * 512], BF16)
    masktp_in = ein("masktp", [128, 4 * 512], BF16)
    ropecq = ein("ropecq", [128, S_ALL], BF16); ropesq = ein("ropesq", [128, S_ALL], BF16)
    ropeck = ein("ropeck", [128, TOK], BF16); ropesk = ein("ropesk", [128, TOK], BF16)

    yout = nc.dram_tensor("y", [DM, TOK], F32, kind="ExternalOutput").ap()

    g_in = nc.dram_tensor("g_in", [1, 4], F32).ap()
    g_out = nc.dram_tensor("g_out", [1, 4], F32).ap()
    agw_in = [nc.dram_tensor(f"agw_in{i}", [1, ROWW], BF16).ap() for i in range(3)]
    agw_out = [nc.dram_tensor(f"agw_out{i}", [8, ROWW], BF16, addr_space="Shared").ap() for i in range(3)]
    agl_in = nc.dram_tensor("agl_in", [1, ROWL], BF16).ap()
    agl_out = nc.dram_tensor("agl_out", [8, ROWL], BF16, addr_space="Shared").ap()
    a2a_in = nc.dram_tensor("a2a_in", [8, A2AB], BF16).ap()
    a2a_out = nc.dram_tensor("a2a_out", [8, A2AB], BF16).ap()

    dbg = {}
    if DBG:
        for nm, shp, dt in [("dbg_resid1", [128, KD * TOK], F32),
                            ("dbg_ln2", [128, KD * TOK], BF16),
                            ("dbg_agl", [8, ROWL], BF16),
                            ("dbg_qn0", [128, S_ALL], BF16),
                            ("dbg_qrp", [128, S_ALL], BF16),
                            ("dbg_kn", [128, S_ALL], BF16),
                            ("dbg_v", [128, 64 * 132], BF16),
                            ("dbg_a2a", [8, A2AB], BF16),
                            ("dbg_resid2", [128, KD * TOK], F32),
                            ("dbg_qw", [8, ROWW], BF16),
                            ("dbg_lnx", [128, KD * TOK], BF16),
                            ("dbg_h", [128, HT * 512], BF16),
                            ("dbg_gthr", [128, 8], F32),
                            ("dbg_redc", [128, 32], F32)]:
            dbg[nm] = nc.dram_tensor(nm, shp, dt, kind="ExternalOutput").ap()

    SBASE = 16512
    STOP = 229344
    off = [0]

    def sat(name, shape, dt, at=None):
        a = off[0] if at is None else at
        th = nc.alloc_sbuf_tensor_at(name, shape, dt, offset=SBASE + a)
        esz = 2 if dt == BF16 else 4
        end = a + int(np.prod(shape[1:])) * esz
        assert SBASE + end <= STOP, f"{name} overflows sbuf: {SBASE+end}"
        if at is None:
            off[0] = (end + 31) // 32 * 32
        return th

    resid = sat("resid", [128, KD * TOK], F32)
    lnbuf = sat("lnbuf", [128, KD * TOK], BF16)
    onesr = sat("onesr_sb", [1, 128], F32)
    onesrow_b = sat("onesrow_b", [1, 256], BF16)
    onescol_f = sat("onescol_f", [128, 8], F32)
    id_f = sat("id_f", [128, 128], F32)
    id_b = sat("id_b", [128, 128], BF16)
    pmT = sat("pmT_sb", [128, 128], BF16)
    onecol_b = sat("onecol_b", [128, 16], BF16)
    qnw = sat("qnw_sb", [128, QKD], F32)
    kvnw = sat("kvnw_sb", [128, KVKD], F32)
    cell_ln = sat("cell_ln", [1, 8], F32)
    cell_rq = sat("cell_rq", [1, 8], F32)
    mcol = sat("mcol", [128, 32], F32)
    mbuf = sat("mbuf", [128, 8], F32)
    rlbuf = sat("rlbuf", [128, 8], F32)
    redc = sat("redc", [128, 32], F32)
    nmrow4 = sat("nmrow4", [1, 1024], BF16)
    mqd_bf = sat("mqd_bf", [128, 16], F32)
    m_row = sat("m_row", [1, 1024], F32)
    rstd_row = sat("rstd_row", [1, 1024], F32)
    srow = sat("srow", [1, 1024], F32)
    maskp = sat("maskp_sb", [128, 4 * 512], BF16)
    masktp = sat("masktp_sb", [128, 4 * 512], BF16)
    gthr = sat("gthr", [128, 8], F32)
    zcol = sat("zcol", [128, 8], F32)
    tmpf = sat("tmpf", [128, 512], F32)
    sqstage = sat("sqstage", [128, 1024], BF16)
    AREN = off[0]
    AREN_SZ = (STOP - SBASE) - AREN
    assert AREN_SZ >= 78848, f"arena too small: {AREN_SZ}"

    qch = [sat(f"qch{i}", [128, 1408], F32, at=AREN + i * 5632) for i in range(2)]
    qabs = sat("qabs", [128, 1408], F32, at=AREN + 11264)
    qsgn = sat("qsgn", [128, 1408], BF16, at=AREN + 16896)
    qq = [sat(f"qq{i}", [128, 1408], BF16, at=AREN + 19712 + i * 2816) for i in range(2)]
    h_buf = sat("h_buf", [128, HT * 512], BF16, at=AREN)
    qg1 = [sat(f"qg1_{i}", [128, 16 * 256], BF16, at=AREN + 45056 + i * 8192) for i in range(2)]
    qg2 = [sat(f"qg2_{i}", [128, 16 * 256], BF16, at=AREN + 61440 + i * 8192) for i in range(2)]
    q3b = [sat(f"q3b{i}", [128, HID], BF16, at=AREN + 45056 + i * 11264) for i in range(2)]
    sgstage = sat("sgstage", [128, 1024], BF16, at=AREN + 77824)
    qc_raw = sat("qc_raw", [128, QKD * TOK], BF16, at=AREN)
    kvc_raw = sat("kvc_raw", [128, KVKD * TOK], BF16, at=AREN + 24576)
    wdn_buf = [sat(f"wdn{i}", [128, 16 * 512], BF16, at=AREN + 32768 + i * 16384) for i in range(2)]
    wkr_buf = sat("wkr_buf", [128, 16 * 64], BF16, at=AREN + 65536)
    ck_sb = sat("ck_sb", [128, TOK], BF16, at=AREN + 67584)
    sk_sb = sat("sk_sb", [128, TOK], BF16, at=AREN + 69632)
    kr1 = sat("kr1", [128, 1024], BF16, at=AREN + 71680)
    kr2 = sat("kr2", [128, 1024], BF16, at=AREN + 73728)
    q_n = [sat(f"q_n{j}", [128, S_ALL], BF16, at=AREN + j * 16384) for j in range(2)]
    k_nope = sat("k_nope", [128, S_ALL], BF16, at=AREN + 32768)
    v_buf = sat("v_buf", [128, 64 * 132], BF16, at=AREN + 49152)
    kvk_sb = sat("kvk_sb", [128, KVKD * 256], BF16, at=AREN + 66048)
    kvv_sb = sat("kvv_sb", [128, KVKD * 256], BF16, at=AREN + 68096)
    kvwin = [sat(f"kvwin{i}", [128, KVKD * 512], BF16, at=AREN + 70144 + i * 4096) for i in range(2)]
    pT = [sat(f"pT{i}", [128, 512], BF16, at=AREN + 78336 + i * 1024) for i in range(2)]
    o_stage = sat("o_stage", [128, 1024], BF16, at=AREN + 80384)
    qcwin = [sat(f"qcwin{i}", [128, QKD * 512], BF16, at=AREN + 32768 + i * 12288) for i in range(2)]
    whead = sat("whead", [128, QKD * 384], BF16, at=AREN + 57344)
    tabc = [sat(f"tabc{i}", [128, 512], BF16, at=AREN + 66560 + i * 1024) for i in range(2)]
    tabs = [sat(f"tabs{i}", [128, 512], BF16, at=AREN + 68608 + i * 1024) for i in range(2)]
    qrps = sat("qrps", [128, 512], BF16, at=AREN + 70656)
    tabt = sat("tabt", [128, 512], BF16, at=AREN + 71680)
    LOFF = 65536
    krope_all = sat("krope_all", [128, S_ALL], BF16, at=LOFF)
    q_rp = sat("q_rp", [128, S_ALL], BF16, at=LOFF + 16384)
    wo_buf = [sat(f"wo_buf{i}", [128, 16 * 512], BF16, at=AREN + i * 16384) for i in range(2)]
    aow = [sat(f"aow{i}", [128, 16 * 512], BF16, at=AREN + 32768 + i * 16384) for i in range(2)]

    ctx = contextlib.ExitStack()
    pb = [ctx.enter_context(nc.psum_tensor(f"pb{i}", [128, 512], F32)) for i in range(7)]
    pbT = ctx.enter_context(nc.psum_tensor("pbT", [128, 1024], BF16))

    SY, SG, ST_, SS, SV, SC = "sy", "sg", "st", "ss", "sv", "sc"

    D("y", resid[:, :], xT.rearrange("(kd p) t -> p kd t", p=128), SY)
    D("y", onesr[0:1, :], onesr_in, SY)
    D("y", id_f[:, :], idf_in, SY)
    D("y", id_b[:, :], idb_in, SY)
    D("y", pmT[:, :], pmT_in, SY)
    D("y", qnw[:, :], qnw_t, SY)
    D("y", kvnw[:, :], kvnw_t, SY)
    D("y", maskp[:, :], maskp_in, SY)
    D("y", masktp[:, :], masktp_in, SY)
    E("g", lambda g: g.memset(onecol_b[:, :], 1.0))
    E("g", lambda g: g.memset(onesrow_b[:, :], 1.0))
    E("g", lambda g: g.memset(onescol_f[:, :], 1.0))
    E("g", lambda g: g.memset(cell_ln[:, :], 1e-5))
    E("g", lambda g: g.memset(cell_rq[:, :], 1e-6))
    E("g", lambda g: g.memset(zcol[:, :], 0.0), inc=SG)
    B.barrier()

    B.phase = "P1_quant"
    if PHASE_LIMIT >= 1:
        w12v = [w1_s.rearrange("(a p) h -> a p h", p=128),
                w2_s.rearrange("(a p) h -> a p h", p=128)]
        w3v = w3_s.rearrange("(a p) d -> p a d", p=128)
        agwv12 = [agw_in[i].rearrange("o (a p h) -> o a p h", a=2, p=128) for i in range(2)]
        agwv3 = agw_in[2].rearrange("o (a p d) -> o p a d", a=44, p=128)

        def chunk_src(mi, chn):
            if mi < 2:
                return w12v[mi][chn // 4, :, (chn % 4) * 1408:(chn % 4 + 1) * 1408]
            return w3v[:, 4 * chn:4 * chn + 4, :]

        NCH = [8, 8, 11]
        CW = [1408, 1408, 1024]
        ycons = []
        for mi in range(3):
            for chn in range(NCH[mi]):
                bi = chn % 2
                if len(ycons) >= 2:
                    W("y", SV, ycons[-2])
                D("y", qch[bi][:, 0:CW[mi]], chunk_src(mi, chn), f"ya{bi}")
                W("v", f"ya{bi}")
                col = [0, 8, 16][mi] + chn
                E("v", (lambda bi_, col_, w_: lambda v: v.reduce_sum(
                    redc[:, col_:col_ + 1], qch[bi_][:, 0:w_], axis=AX.X,
                    apply_absolute_value=True))(bi, col, CW[mi]), inc=SV)
                ycons.append(M(SV))
        W("v", SV)
        for mi in range(3):
            E("v", (lambda mi_, n_: lambda v: v.reduce_sum(
                redc[:, 28 + mi_:29 + mi_], redc[:, [0, 8, 16][mi_]:[0, 8, 16][mi_] + n_], axis=AX.X))(mi, NCH[mi]),
              inc=SV)
        W("t", SV)
        E("t", lambda t: t.matmul(pb[6][0:1, 0:4], onescol_f[:, 0:1], redc[:, 28:32],
                                  start=True, stop=True), inc=ST_)
        W("v", ST_)
        E("v", lambda v: v.tensor_copy(srow[0:1, 0:4], pb[6][0:1, 0:4]), inc=SV)
        W("y", SV)
        D("y", g_in, srow[0:1, 0:4], SY)
        W("g", SY)
        E("g", lambda g: g.collective_compute(
            "AllReduce", ALU.add, replica_groups=[list(range(NCORE))],
            ins=[g_in.opt()], outs=[g_out.opt()]), inc=SC)
        W("y", SC)
        D("y", srow[0:1, 0:4], g_out, SY)
        W("v", SY)
        E("v", lambda v: v.tensor_scalar(srow[0:1, 4:8], srow[0:1, 0:4], -0.5 / WCNT, -0.5e-8,
                                         op0=ALU.mult, op1=ALU.add), inc=SV)
        E("v", lambda v: v.tensor_scalar(srow[0:1, 0:4], srow[0:1, 0:4], 0.5 / WCNT, 0.5e-8,
                                         op0=ALU.mult, op1=ALU.add), inc=SV)
        W("t", SV)
        E("t", lambda t: t.matmul(pb[6][:, 0:8], onesr[0:1, :], srow[0:1, 0:8],
                                  start=True, stop=True), inc=ST_)
        W("v", ST_)
        E("v", lambda v: v.tensor_copy(gthr[:, 0:8], pb[6][:, 0:8]), inc=SV)
        B.barrier()
        for mi in range(3):
            dmamarks = []
            vq_marks = []
            for chn in range(NCH[mi]):
                bi = chn % 2
                wd = CW[mi]
                if len(vq_marks) >= 2:
                    W("y", SV, vq_marks[-2])
                D("y", qch[bi][:, 0:wd], chunk_src(mi, chn), f"yb{bi}")
                if vq_marks:
                    W("s", SV, vq_marks[-1])
                W("s", f"yb{bi}")
                W("v", f"yb{bi}")
                E("s", (lambda bi_, w_: lambda s: s.sign(qsgn[:, 0:w_], qch[bi_][:, 0:w_]))(bi, wd))
                E("s", (lambda bi_, w_: lambda s: s.activation(
                    qabs[:, 0:w_], qch[bi_][:, 0:w_], AF.Abs))(bi, wd), inc=SS)
                W("s", SS)
                E("s", (lambda w_, mi_: lambda s: s.sign(
                    qabs[:, 0:w_], qabs[:, 0:w_], bias=gthr[:, 4 + mi_:5 + mi_]))(wd, mi), inc=SS)
                if len(dmamarks) >= 2:
                    W("v", SY, dmamarks[-2])
                W("v", SS)
                E("v", (lambda w_: lambda v: v.tensor_relu(qabs[:, 0:w_], qabs[:, 0:w_]))(wd), inc=SV)
                W("v", SV)
                E("v", (lambda bi_, w_: lambda v: v.tensor_mul(
                    qq[bi_][:, 0:w_], qabs[:, 0:w_], qsgn[:, 0:w_]))(bi, wd), inc=SV)
                vq_marks.append(M(SV))
                W("y", SV)
                if mi < 2:
                    dst = agwv12[mi][0, chn // 4, :, (chn % 4) * 1408:(chn % 4 + 1) * 1408]
                    D("y", dst, qq[bi][:, 0:wd], SY)
                else:
                    dst = agwv3[0, :, 4 * chn:4 * chn + 4, :]
                    D("y", dst, qq[bi][:, 0:1024].rearrange("p (e d) -> p e d", e=4), SY)
                dmamarks.append(M(SY))
            W("g", SY)
            E("g", (lambda mi_: lambda g: g.collective_compute(
                "AllGather", ALU.bypass, replica_groups=[list(range(NCORE))],
                ins=[agw_in[mi_].opt()], outs=[agw_out[mi_].opt()]))(mi), inc=SC)
        B.barrier()

    def ln_phase():
        for kd in range(KD):
            W("v", ST_)
            E("v", (lambda kd_: lambda v: v.tensor_copy(
                lnbuf[:, kd_ * 1024:(kd_ + 1) * 1024], resid[:, kd_ * 1024:(kd_ + 1) * 1024]))(kd), inc=SV)
            W("v", SV)
            E("v", (lambda kd_: lambda v: v.tensor_mul(
                sqstage[:, :], lnbuf[:, kd_ * 1024:(kd_ + 1) * 1024],
                lnbuf[:, kd_ * 1024:(kd_ + 1) * 1024]))(kd), inc=SV)
            W("t", SV)
            st0, sp0 = (kd == 0), (kd == KD - 1)
            for half in range(2):
                E("t", (lambda kd_, h_, s0, s1: lambda t: t.matmul(
                    pb[2 + h_][0:1, 0:512], onecol_b[:, 0:1],
                    lnbuf[:, kd_ * 1024 + h_ * 512:kd_ * 1024 + h_ * 512 + 512],
                    start=s0, stop=s1))(kd, half, st0, sp0))
                E("t", (lambda h_, s0, s1: lambda t: t.matmul(
                    pb[4 + h_][0:1, 0:512], onecol_b[:, 0:1],
                    sqstage[:, h_ * 512:h_ * 512 + 512],
                    start=s0, stop=s1))(half, st0, sp0), inc=(ST_ if (half == 1) else None))
        W("v", ST_)
        E("v", lambda v: v.tensor_scalar(m_row[0:1, 0:512], pb[2][0:1, 0:512], 1.0 / DM, None, op0=ALU.mult))
        E("v", lambda v: v.tensor_scalar(m_row[0:1, 512:1024], pb[3][0:1, 0:512], 1.0 / DM, None, op0=ALU.mult))
        E("v", lambda v: v.tensor_scalar(srow[0:1, 0:512], pb[4][0:1, 0:512], 1.0 / DM, None, op0=ALU.mult))
        E("v", lambda v: v.tensor_scalar(srow[0:1, 512:1024], pb[5][0:1, 0:512], 1.0 / DM, None, op0=ALU.mult), inc=SV)
        W("v", SV)
        E("v", lambda v: v.tensor_mul(rstd_row[0:1, :], m_row[0:1, :], m_row[0:1, :]), inc=SV)
        W("v", SV)
        E("v", lambda v: v.tensor_sub(rstd_row[0:1, :], srow[0:1, :], rstd_row[0:1, :]), inc=SV)
        W("s", SV)
        E("s", lambda s: s.activation(rstd_row[0:1, :], rstd_row[0:1, :], AF.Ln, bias=cell_ln[0:1, 0:1]))
        E("s", lambda s: s.activation(rstd_row[0:1, :], rstd_row[0:1, :], AF.Exp, scale=-0.5), inc=SS)
        for half in range(2):
            W("t", SS)
            if half == 1:
                W("t", SV)
            E("t", (lambda h_: lambda t: t.matmul(
                pb[6][:, 0:512], onesr[0:1, :], m_row[0:1, h_ * 512:h_ * 512 + 512],
                start=True, stop=True))(half))
            E("t", (lambda h_: lambda t: t.matmul(
                pb[0][:, 0:512], onesr[0:1, :], rstd_row[0:1, h_ * 512:h_ * 512 + 512],
                start=True, stop=True))(half), inc=ST_)
            W("v", ST_)
            for kd in range(KD):
                c0 = kd * 1024 + half * 512
                E("v", (lambda c_: lambda v: v.tensor_sub(
                    tmpf[:, :], resid[:, c_:c_ + 512], pb[6][:, 0:512]))(c0), inc=SV)
                W("v", SV)
                E("v", (lambda c_: lambda v: v.tensor_mul(
                    lnbuf[:, c_:c_ + 512], tmpf[:, :], pb[0][:, 0:512]))(c0), inc=SV)
        B.barrier()

    def ffn_phase(first):
        agq1 = agw_out[0].rearrange("r (a p h) -> p r a h", a=2, p=128)
        agq2 = agw_out[1].rearrange("r (a p h) -> p r a h", a=2, p=128)
        agq3 = agw_out[2].rearrange("r (a p d) -> r p a d", a=44, p=128)
        if first:
            W("y", SC, 3)
        for half in range(2):
            hoff = half * 512
            sig_marks = []
            hv_marks = []
            for mg in range(22):
                bi = mg % 2
                if mg == 0:
                    W("y", ST_, M(ST_))
                if mg >= 2:
                    W("y", ST_, qg_cons[bi])
                D("y", qg1[bi][:, :].rearrange("p (r a hh) -> p r a hh", r=8, a=2),
                  agq1[:, :, :, mg * 256:(mg + 1) * 256], f"yq{bi}")
                D("y", qg2[bi][:, :].rearrange("p (r a hh) -> p r a hh", r=8, a=2),
                  agq2[:, :, :, mg * 256:(mg + 1) * 256], f"yq{bi}")
                W("t", f"yq{bi}")
                for mt in range(2):
                    ht = mg * 2 + mt
                    pA = pb[ht % 2]
                    pB = pb[2 + ht % 2]
                    if ht >= 2:
                        W("t", SS, sig_marks[ht - 2])
                        W("t", SV, hv_marks[ht - 2])
                    for kdi in range(KD):
                        E("t", (lambda buf_, k_, m_, pA_, h0_: lambda t: t.matmul(
                            pA_[:, 0:512], buf_[:, k_ * 256 + m_ * 128:k_ * 256 + m_ * 128 + 128],
                            lnbuf[:, k_ * 1024 + h0_:k_ * 1024 + h0_ + 512],
                            start=(k_ == 0), stop=(k_ == KD - 1)))(qg1[bi], kdi, mt, pA, hoff),
                          inc=(ST_ if kdi == KD - 1 else None))
                    for kdi in range(KD):
                        E("t", (lambda buf_, k_, m_, pB_, h0_: lambda t: t.matmul(
                            pB_[:, 0:512], buf_[:, k_ * 256 + m_ * 128:k_ * 256 + m_ * 128 + 128],
                            lnbuf[:, k_ * 1024 + h0_:k_ * 1024 + h0_ + 512],
                            start=(k_ == 0), stop=(k_ == KD - 1)))(qg2[bi], kdi, mt, pB, hoff),
                          inc=(ST_ if kdi == KD - 1 else None))
                    W("s", ST_, M(ST_) - 1)
                    if ht >= 2:
                        W("s", SV, hv_marks[ht - 2])
                    E("s", (lambda pA_, ht_: lambda s: s.activation(
                        sgstage[:, (ht_ % 2) * 512:(ht_ % 2) * 512 + 512],
                        pA_[:, 0:512], AF.Sigmoid))(pA, ht), inc=SS)
                    sig_marks.append(M(SS))
                    W("v", SS)
                    W("v", ST_)
                    E("v", (lambda pB_, ht_: lambda v: v.tensor_mul(
                        h_buf[:, ht_ * 512:ht_ * 512 + 512],
                        sgstage[:, (ht_ % 2) * 512:(ht_ % 2) * 512 + 512], pB_[:, 0:512]))(pB, ht),
                      inc=SV)
                    hv_marks.append(M(SV))
                if mg == 0:
                    qg_cons = [None, None]
                qg_cons[bi] = M(ST_)
            h_done = M(SV)
            x12_done = M(ST_)
            q3_cons = [None, None]
            res_marks = []
            for dt in range(KD):
                bi = dt % 2
                if dt == 0:
                    if first and half == 0:
                        W("y", SC, 4)
                    W("y", ST_, x12_done)
                if dt >= 2:
                    W("y", ST_, q3_cons[bi])
                D("y", q3b[bi][:, :].rearrange("p (a dd) -> p a dd", a=44),
                  agq3[dt // 2, :, :, (dt % 2) * 128:(dt % 2) * 128 + 128], f"y3{bi}")
                W("t", f"y3{bi}")
                if dt == 0:
                    W("t", SV, h_done)
                pC = pb[4 + dt % 2]
                if dt >= 2:
                    W("t", SV, res_marks[dt - 2])
                for a in range(HT):
                    E("t", (lambda b_, a_, pC_: lambda t: t.matmul(
                        pC_[:, 0:512], q3b[b_][:, a_ * 128:a_ * 128 + 128],
                        h_buf[:, a_ * 512:a_ * 512 + 512],
                        start=(a_ == 0), stop=(a_ == HT - 1)))(bi, a, pC),
                      inc=(ST_ if a == HT - 1 else None))
                q3_cons[bi] = M(ST_)
                W("v", ST_)
                E("v", (lambda pC_, c_: lambda v: v.scalar_tensor_tensor(
                    resid[:, c_:c_ + 512], pC_[:, 0:512], 0.5, resid[:, c_:c_ + 512],
                    op0=ALU.mult, op1=ALU.add))(pC, dt * 1024 + hoff), inc=SV)
                res_marks.append(M(SV))
        B.barrier()

    if PHASE_LIMIT >= 2:
        B.phase = "P2_ln1"
        ln_phase()
        B.phase = "P2_ffn1"
        if DBG:
            D("y", dbg["dbg_lnx"], lnbuf[:, :], SY)
            B.barrier()
        ffn_phase(first=True)
        if DBG:
            D("y", dbg["dbg_resid1"], resid[:, :], SY)
            W("y", SC)
            D("y", dbg["dbg_qw"], agw_out[0], SY)
            D("y", dbg["dbg_gthr"], gthr[:, :], SY)
            D("y", dbg["dbg_redc"], redc[:, :], SY)
            D("y", dbg["dbg_h"], h_buf[:, :], SY)
    if PHASE_LIMIT >= 3:
        B.phase = "P3_ln2"
        ln_phase()
        if DBG:
            D("y", dbg["dbg_ln2"], lnbuf[:, :], SY)
            B.barrier()

    B.phase = "P4_proj"
    if PHASE_LIMIT >= 4:
        wqdv = wq_down.rearrange("(kd p) m -> p kd m", p=128)
        wkvdv = wkv_down.rearrange("(kd p) m -> p kd m", p=128)
        wkrv = wk_rope.rearrange("(kd p) m -> p kd m", p=128)
        aglq = agl_in.rearrange("o (a p t) -> o a p t", a=QR // 128 + KVR // 128 + 1, p=128)

        def proj_rms(nt, wview, mgroups, statbank, meann, rawbuf):
            wdn_cons = [0, 0]
            W("v", SY)
            for mg in range(mgroups):
                bi = mg % 2
                if mg >= 2:
                    W("g", ST_, wdn_cons[bi])
                D("g", wdn_buf[bi][:, :].rearrange("p (kd mm) -> p kd mm", kd=16),
                  wview[:, :, mg * 512:(mg + 1) * 512], f"gw{bi}")
                W("t", f"gw{bi}")
                for mt in range(4):
                    tile = mg * 4 + mt
                    if tile >= nt:
                        break
                    for half in range(2):
                        pq = pb[(tile * 2 + half) % 2]
                        if tile * 2 + half >= 2:
                            W("t", SS, M(SS) - 1)
                        for kdi in range(KD):
                            E("t", (lambda b_, k_, m_, p_, h_: lambda t: t.matmul(
                                p_[:, 0:512], wdn_buf[b_][:, k_ * 512 + m_ * 128:k_ * 512 + m_ * 128 + 128],
                                lnbuf[:, k_ * 1024 + h_ * 512:k_ * 1024 + h_ * 512 + 512],
                                start=(k_ == 0), stop=(k_ == KD - 1)))(bi, kdi, mt, pq, half),
                              inc=(ST_ if kdi == KD - 1 else None))
                        W("s", ST_)
                        E("s", (lambda p_, c_: lambda s: s.copy(
                            rawbuf[:, c_:c_ + 512], p_[:, 0:512]))(pq, tile * 1024 + half * 512), inc=SS)
                wdn_cons[bi] = M(ST_)
            W("v", SS)
            stat_marks = []
            sv_marks = []
            for tile in range(nt):
                for half in range(2):
                    i = tile * 2 + half
                    if i >= 2:
                        W("v", ST_, stat_marks[i - 2])
                    E("v", (lambda c_, sl_: lambda v: v.tensor_mul(
                        sqstage[:, sl_ * 512:sl_ * 512 + 512], rawbuf[:, c_:c_ + 512],
                        rawbuf[:, c_:c_ + 512]))(tile * 1024 + half * 512, i % 2), inc=SV)
                    sv_marks.append(M(SV))
                    W("t", SV, sv_marks[-1])
                    E("t", (lambda h_, s0, s1, sb_, sl_: lambda t: t.matmul(
                        pb[sb_ + h_][0:1, 0:512], onecol_b[:, 0:1],
                        sqstage[:, sl_ * 512:sl_ * 512 + 512],
                        start=s0, stop=s1))(half, tile == 0, tile == nt - 1, statbank, i % 2),
                      inc=ST_)
                    stat_marks.append(M(ST_))
            W("s", ST_)
            E("s", (lambda sb_, mn_: lambda s: s.activation(
                rstd_row[0:1, 0:512], pb[sb_][0:1, 0:512], AF.Ln, bias=cell_rq[0:1, 0:1],
                scale=1.0 / mn_))(statbank, meann))
            E("s", (lambda sb_, mn_: lambda s: s.activation(
                rstd_row[0:1, 512:1024], pb[sb_ + 1][0:1, 0:512], AF.Ln, bias=cell_rq[0:1, 0:1],
                scale=1.0 / mn_))(statbank, meann))
            E("s", lambda s: s.activation(rstd_row[0:1, :], rstd_row[0:1, :], AF.Exp, scale=-0.5), inc=SS)
            for half in range(2):
                W("t", SS)
                if half == 1:
                    W("t", SY)
                E("t", (lambda h_: lambda t: t.matmul(
                    pb[6][:, 0:512], onesr[0:1, :], rstd_row[0:1, h_ * 512:h_ * 512 + 512],
                    start=True, stop=True))(half), inc=ST_)
                W("v", ST_)
                for tile in range(nt):
                    stg = sqstage[:, (tile % 2) * 512:(tile % 2) * 512 + 512]
                    if tile >= 2:
                        W("v", SY, M(SY) - 16)
                    E("v", (lambda stg_, c_: lambda v: v.tensor_mul(
                        stg_, rawbuf[:, c_:c_ + 512], pb[6][:, 0:512]))(stg, tile * 1024 + half * 512),
                      inc=SV)
                    W("y", SV)
                    D("y", (lambda t_, h_: aglq[0, t_ + (0 if rawbuf is qc_raw else QR // 128), :,
                                                h_ * 512:h_ * 512 + 512])(tile, half), stg, SY)

        proj_rms(QKD, wqdv, 3, 2, QR, qc_raw)
        B.barrier()
        proj_rms(KVKD, wkvdv, 1, 2, KVR, kvc_raw)
        B.barrier()
        D("y", ck_sb[:, :], ropeck, SY)
        D("y", sk_sb[:, :], ropesk, SY)
        D("g", wkr_buf[:, :].rearrange("p (kd mm) -> p kd mm", kd=16), wkrv, SG)
        W("t", SG)
        for half in range(2):
            for base in (0, 64):
                for kdi in range(KD):
                    E("t", (lambda k_, b_, h_: lambda t: t.matmul(
                        pb[1][b_:b_ + 64, 0:512], wkr_buf[:, k_ * 64:k_ * 64 + 64],
                        lnbuf[:, k_ * 1024 + h_ * 512:k_ * 1024 + h_ * 512 + 512],
                        start=(k_ == 0), stop=(k_ == KD - 1)))(kdi, base, half),
                      inc=(ST_ if kdi == KD - 1 else None))
            W("s", ST_)
            E("s", (lambda h_: lambda s: s.copy(kr1[:, h_ * 512:h_ * 512 + 512], pb[1][:, 0:512]))(half),
              inc=SS)
            W("t", SS)
            E("t", (lambda h_: lambda t: t.matmul(
                pb[0][:, 0:512], pmT[:, :], kr1[:, h_ * 512:h_ * 512 + 512],
                start=True, stop=True))(half), inc=ST_)
            W("v", SS)
            E("v", (lambda h_: lambda v: v.tensor_mul(
                kr2[:, h_ * 512:h_ * 512 + 512], kr1[:, h_ * 512:h_ * 512 + 512],
                ck_sb[:, h_ * 512:h_ * 512 + 512]))(half))
            W("v", ST_)
            E("v", (lambda h_: lambda v: v.tensor_mul(
                kr1[:, h_ * 512:h_ * 512 + 512], pb[0][:, 0:512],
                sk_sb[:, h_ * 512:h_ * 512 + 512]))(half), inc=SV)
            W("v", SV)
            E("v", (lambda h_: lambda v: v.tensor_add(
                kr2[:, h_ * 512:h_ * 512 + 512], kr2[:, h_ * 512:h_ * 512 + 512],
                kr1[:, h_ * 512:h_ * 512 + 512]))(half), inc=SV)
        W("y", SV)
        D("y", aglq[0, QR // 128 + KVR // 128, :, :], kr2[:, :], SY)
        W("g", SY)
        E("g", lambda g: g.collective_compute(
            "AllGather", ALU.bypass, replica_groups=[list(range(NCORE))],
            ins=[agl_in.opt()], outs=[agl_out.opt()]), inc=SC)
        B.barrier()
        if DBG:
            W("y", SC)
            D("y", dbg["dbg_agl"], agl_out, SY)
            B.barrier()

    B.phase = "P5a_qbuild"
    if PHASE_LIMIT >= 5:
        W("y", SC)
        aglqv = agl_out[:, 0:QOFF].rearrange("r (a p t) -> r p a t", p=128, t=1024)
        aglkv = agl_out[:, QOFF:KVOFF].rearrange("r (a p t) -> r p a t", p=128, t=1024)
        aglkr = agl_out[:, KVOFF:ROWL].rearrange("r (p t) -> p r t", p=128)
        D("g", whead[:, :].rearrange("p (kd mm) -> p kd mm", kd=QKD),
          wq_head.rearrange("(kd p) m -> p kd m", p=128), SG)
        W("v", SG)
        for kdi in range(QKD):
            E("v", (lambda k_: lambda v: v.tensor_scalar(
                whead[:, k_ * 384:k_ * 384 + 384], whead[:, k_ * 384:k_ * 384 + 384],
                qnw[:, k_:k_ + 1], None, op0=ALU.mult))(kdi), inc=(SV if kdi == QKD - 1 else None))
        D("y", krope_all[:, :].rearrange("p (r t) -> p r t", r=8), aglkr, SY)
        W("t", SV)
        qc_cons = [0, 0]
        rp_cons = [0, 0]
        for w in range(16):
            bi = w % 2
            if w >= 2:
                W("y", ST_, qc_cons[bi])
                W("y", SV, rp_cons[bi])
            D("y", qcwin[bi][:, :].rearrange("p (a tt) -> p a tt", a=QKD),
              aglqv[w // 2, :, :, (w % 2) * 512:(w % 2) * 512 + 512], f"yc{bi}")
            D("y", tabc[bi][:, :], ropecq[:, w * 512:(w + 1) * 512], f"yc{bi}")
            D("y", tabs[bi][:, :], ropesq[:, w * 512:(w + 1) * 512], f"yc{bi}")
            W("t", f"yc{bi}")
            W("v", f"yc{bi}")
            for (mi, pdst) in ((0, pb[0]), (1, pb[1]), (2, pb[2])):
                for kdi in range(QKD):
                    E("t", (lambda b_, k_, m_, p_: lambda t: t.matmul(
                        p_[:, 0:512], whead[:, k_ * 384 + m_ * 128:k_ * 384 + m_ * 128 + 128],
                        qcwin[b_][:, k_ * 512:k_ * 512 + 512],
                        start=(k_ == 0), stop=(k_ == QKD - 1)))(bi, kdi, mi, pdst),
                      inc=(ST_ if kdi == QKD - 1 else None))
            W("s", ST_)
            E("s", (lambda w_: lambda s: s.activation(
                q_n[0][:, w_ * 512:w_ * 512 + 512], pb[0][:, 0:512], AF.Copy, scale=SCALE))(w))
            E("s", (lambda w_: lambda s: s.activation(
                q_n[1][:, w_ * 512:w_ * 512 + 512], pb[1][:, 0:512], AF.Copy, scale=SCALE))(w))
            E("s", lambda s: s.copy(qrps[:, :], pb[2][:, 0:512]), inc=SS)
            W("t", SS)
            E("t", lambda t: t.matmul(pb[3][:, 0:512], pmT[:, :], qrps[:, :],
                                      start=True, stop=True), inc=ST_)
            W("v", SS)
            E("v", (lambda b_: lambda v: v.tensor_mul(tabt[:, :], qrps[:, :], tabc[b_][:, :]))(bi))
            W("v", ST_)
            E("v", (lambda b_: lambda v: v.tensor_mul(qrps[:, :], pb[3][:, 0:512], tabs[b_][:, :]))(bi), inc=SV)
            W("v", SV)
            E("v", (lambda w_: lambda v: v.tensor_add(
                q_rp[:, w_ * 512:w_ * 512 + 512], tabt[:, :], qrps[:, :]))(w), inc=SV)
            qc_cons[bi] = M(ST_)
            rp_cons[bi] = M(SV)
        B.barrier()
        if DBG:
            D("y", dbg["dbg_qn0"], q_n[0][:, :], SY)
            D("y", dbg["dbg_qrp"], q_rp[:, :], SY)
            B.barrier()

        D("g", kvk_sb[:, :].rearrange("p (kd mm) -> p kd mm", kd=KVKD),
          wkv_k.rearrange("(kd p) m -> p kd m", p=128), SG)
        D("g", kvv_sb[:, :].rearrange("p (kd mm) -> p kd mm", kd=KVKD),
          wkv_v.rearrange("(kd p) m -> p kd m", p=128), SG)
        W("v", SG)
        for kdi in range(KVKD):
            E("v", (lambda k_: lambda v: v.tensor_scalar(
                kvk_sb[:, k_ * 256:k_ * 256 + 256], kvk_sb[:, k_ * 256:k_ * 256 + 256],
                kvnw[:, k_:k_ + 1], None, op0=ALU.mult))(kdi))
            E("v", (lambda k_: lambda v: v.tensor_scalar(
                kvv_sb[:, k_ * 256:k_ * 256 + 256], kvv_sb[:, k_ * 256:k_ * 256 + 256],
                kvnw[:, k_:k_ + 1], None, op0=ALU.mult))(kdi), inc=(SV if kdi == KVKD - 1 else None))
        B.barrier()

        o_dma_mark = [None]
        for j in range(2):
            B.phase = f"P5b_kv{j}"
            if j == 0:
                E("g", lambda g: g.memset(v_buf[:, :], 1.0), inc=SG)
                W("s", SG)
            kv_cons = [0, 0]
            for w in range(16):
                bi = w % 2
                if w >= 2:
                    W("y", ST_, kv_cons[bi])
                D("y", kvwin[bi][:, :].rearrange("p (a tt) -> p a tt", a=KVKD),
                  aglkv[w // 2, :, :, (w % 2) * 512:(w % 2) * 512 + 512], f"yk{bi}")
                W("t", f"yk{bi}")
                for kdi in range(KVKD):
                    E("t", (lambda b_, k_, j_: lambda t: t.matmul(
                        pb[0][:, 0:512], kvk_sb[:, k_ * 256 + j_ * 128:k_ * 256 + j_ * 128 + 128],
                        kvwin[b_][:, k_ * 512:k_ * 512 + 512],
                        start=(k_ == 0), stop=(k_ == KVKD - 1)))(bi, kdi, j),
                      inc=(ST_ if kdi == KVKD - 1 else None))
                W("s", ST_)
                E("s", (lambda w_: lambda s: s.copy(
                    k_nope[:, w_ * 512:w_ * 512 + 512], pb[0][:, 0:512]))(w), inc=SS)
                for tt in range(4):
                    pv_ = pb[2 + tt % 2]
                    if w > 0 or tt >= 2:
                        W("t", SS, M(SS) - 1)
                    for kdi in range(KVKD):
                        E("t", (lambda b_, k_, t2, j_, p_: lambda t: t.matmul(
                            p_[:, 0:128], kvwin[b_][:, k_ * 512 + t2 * 128:k_ * 512 + t2 * 128 + 128],
                            kvv_sb[:, k_ * 256 + j_ * 128:k_ * 256 + j_ * 128 + 128],
                            start=(k_ == 0), stop=(k_ == KVKD - 1)))(bi, kdi, tt, j, pv_),
                          inc=(ST_ if kdi == KVKD - 1 else None))
                    W("s", ST_)
                    E("s", (lambda p_, c_: lambda s: s.copy(
                        v_buf[:, c_:c_ + 128], p_[:, 0:128]))(pv_, (4 * w + tt) * 132), inc=SS)
                kv_cons[bi] = M(ST_)
            B.barrier()
            if DBG and j == 0:
                D("y", dbg["dbg_kn"], k_nope[:, :], SY)
                D("y", dbg["dbg_v"], v_buf[:, :], SY)
                B.barrier()

            B.phase = f"P5c_attn{j}"
            s_red_marks = []
            nm_marks = {}
            exp_by_par = [0, 0]
            pv_by_par = [0, 0]
            fin_marks = {"s": 0, "v": 0}

            SKIP_NM = SKIP_RED = SKIP_MASK = False
            def emit_pass1_quad(Q):
                sl2 = (Q % 2) * 4
                for qi in range(4):
                    qt = 4 * Q + qi
                    ph = qt % 4
                    sl = (qt % 2) * 16
                    for kc in range(Q + 1):
                        sb_ = pb[len(s_red_marks) % 2]
                        if len(s_red_marks) >= 2:
                            W("t", SV, s_red_marks[-2])
                        E("t", (lambda j_, q_, k_, p_: lambda t: t.matmul(
                            p_[:, 0:512], q_n[j_][:, q_ * 128:q_ * 128 + 128],
                            k_nope[:, k_ * 512:k_ * 512 + 512], start=True, stop=False))(j, qt, kc, sb_))
                        E("t", (lambda j_, q_, k_, p_: lambda t: t.matmul(
                            p_[:, 0:512], q_rp[64 * j_:64 * j_ + 64, q_ * 128:q_ * 128 + 128],
                            krope_all[64 * j_:64 * j_ + 64, k_ * 512:k_ * 512 + 512],
                            start=False, stop=True))(j, qt, kc, sb_), inc=ST_)
                        W("v", ST_)
                        if kc == Q and not SKIP_MASK:
                            E("v", (lambda p_, ph_: lambda v: v.tensor_add(
                                p_[:, 0:512], p_[:, 0:512], maskp[:, ph_ * 512:ph_ * 512 + 512]))(sb_, ph),
                              inc=SV)
                            W("v", SV)
                        E("v", (lambda p_, k_, sl_: lambda v: v.reduce_max(
                            mcol[:, sl_ + k_:sl_ + k_ + 1], p_[:, 0:512], axis=AX.X))(sb_, kc, sl), inc=SV)
                        s_red_marks.append(M(SV))
                    W("v", SV)
                    if SKIP_RED:
                        continue
                    E("v", (lambda r_, sl_, c_: lambda v: v.reduce_max(
                        mqd_bf[:, c_:c_ + 1], mcol[:, sl_:sl_ + r_], axis=AX.X))(Q + 1, sl, sl2 + qi),
                      inc=SV)
                nm_marks[Q] = M(SS)

            SKIP_PV = SKIP_EXP = SKIP_FIN = SKIP_ST = False
            def emit_pass2_quad(Q):
                q0 = 4 * Q
                W("t", SS, nm_marks[Q])
                W("t", SS, fin_marks["s"])
                W("t", SV, fin_marks["v"])
                sl2 = (Q % 2) * 4
                first = [True]
                for kc in range(Q + 1):
                    for ks in range(4):
                        if SKIP_ST:
                            break
                        par = (kc * 4 + ks) % 2
                        stb = pb[2 + par]
                        if exp_by_par[par]:
                            W("t", SS, exp_by_par[par])
                        E("t", (lambda q_, k_, ks_, p_, j_: lambda t: t.matmul(
                            p_[:, 0:512],
                            k_nope[:, k_ * 512 + ks_ * 128:k_ * 512 + ks_ * 128 + 128],
                            q_n[j_][:, q_ * 512:q_ * 512 + 512], start=True, stop=False))(Q, kc, ks, stb, j))
                        E("t", (lambda q_, k_, ks_, p_, j_: lambda t: t.matmul(
                            p_[:, 0:512],
                            krope_all[64 * j_:64 * j_ + 64, k_ * 512 + ks_ * 128:k_ * 512 + ks_ * 128 + 128],
                            q_rp[64 * j_:64 * j_ + 64, q_ * 512:q_ * 512 + 512],
                            start=False, stop=False))(Q, kc, ks, stb, j))
                        E("t", (lambda p_, sl_: lambda t: t.matmul(
                            p_[:, 0:512], onesrow_b[0:1, 0:128],
                            nmrow4[0:1, sl_ * 128:sl_ * 128 + 512], start=False, stop=True))(stb, sl2),
                          inc=ST_)
                        stm = M(ST_)
                        if kc == Q:
                            W("v", ST_, stm)
                            E("v", (lambda p_, ks_: lambda v: v.tensor_add(
                                p_[:, 0:512], p_[:, 0:512], masktp[:, ks_ * 512:ks_ * 512 + 512]))(stb, ks),
                              inc=SV)
                            W("s", SV)
                        else:
                            W("s", ST_, stm)
                        if pv_by_par[par]:
                            W("s", ST_, pv_by_par[par])
                        if SKIP_EXP:
                            continue
                        E("s", (lambda p_, b_: lambda s: s.activation(
                            pT[b_][:, :], p_[:, 0:512], AF.Exp))(stb, par), inc=SS)
                        em = M(SS)
                        exp_by_par[par] = em
                        W("t", SS, em)
                        if SKIP_PV:
                            continue
                        for qi in range(4):
                            Ob, oc = (pb[4], qi * 132) if qi < 3 else (pb[5], 0)
                            E("t", (lambda b_, k_, ks_, qi_, O_, oc_, s0, s1: lambda t: t.matmul(
                                O_[:, oc_:oc_ + 132], pT[b_][:, qi_ * 128:qi_ * 128 + 128],
                                v_buf[:, (k_ * 4 + ks_) * 132:(k_ * 4 + ks_) * 132 + 132],
                                start=s0, stop=s1))(par, kc, ks, qi, Ob, oc,
                                                    (first[0] and qi in (0, 3)),
                                                    (kc == Q and ks == 3)),
                              inc=(ST_ if qi == 3 else None))
                        first[0] = False
                        pv_by_par[par] = M(ST_)
                if SKIP_FIN or SKIP_PV or SKIP_ST:
                    return
                W("v", ST_, pv_by_par[0])
                W("v", ST_, pv_by_par[1])
                for qi in range(4):
                    Ob, oc = (pb[4], qi * 132) if qi < 3 else (pb[5], 0)
                    E("v", (lambda O_, oc_, qi_: lambda v: v.reciprocal(
                        rlbuf[:, qi_:qi_ + 1], O_[:, oc_ + 128:oc_ + 129]))(Ob, oc, qi),
                      inc=(SV if qi == 3 else None))
                fin_marks["v"] = M(SV)
                W("s", SV)
                for qi in range(4):
                    Ob, oc = (pb[4], qi * 132) if qi < 3 else (pb[5], 0)
                    E("s", (lambda O_, oc_, qi_: lambda s: s.activation(
                        sqstage[:, qi_ * 128:qi_ * 128 + 128], O_[:, oc_:oc_ + 128], AF.Copy,
                        scale=rlbuf[:, qi_:qi_ + 1]))(Ob, oc, qi), inc=(SS if qi == 3 else None))
                fin_marks["s"] = M(SS)
                for qi in range(4):
                    W("t", SS, fin_marks["s"] if qi == 0 else M(SS))
                    E("t", (lambda qi_: lambda t: t.transpose(
                        pbT[:, 0:128], sqstage[:, qi_ * 128:qi_ * 128 + 128], id_b[:, :]))(qi),
                      inc=ST_)
                    W("s", ST_)
                    qt = q0 + qi
                    if qt % 8 == 0 and o_dma_mark[0] is not None:
                        W("s", SY, o_dma_mark[0])
                    E("s", (lambda q_: lambda s: s.copy(
                        o_stage[:, (q_ % 8) * 128:(q_ % 8) * 128 + 128], pbT[:, 0:128]))(qt), inc=SS)
                    if qt % 8 == 7:
                        W("y", SS)
                        D("y", a2a_in[qt // 8:qt // 8 + 1,
                                      j * 131072:(j + 1) * 131072].rearrange("o (p t) -> o p t", p=128)[0],
                          o_stage[:, :], SY)
                        o_dma_mark[0] = M(SY)
                fin_marks["s"] = M(SS)

            NQ = 16
            for Q in range(NQ + 1):
                if Q < NQ:
                    emit_pass1_quad(Q)
                if Q >= 1:
                    emit_pass2_quad(Q - 1)
            B.barrier()
            B.barrier()
        B.phase = "P5d_a2a"
        W("g", SY)
        E("g", lambda g: g.collective_compute(
            "AllToAll", ALU.bypass, replica_groups=[list(range(NCORE))],
            ins=[a2a_in.opt()], outs=[a2a_out.opt()]), inc=SC)
        B.barrier()
        if DBG:
            W("y", SC)
            D("y", dbg["dbg_a2a"], a2a_out, SY)
            B.barrier()

    B.phase = "P6_wo"
    if PHASE_LIMIT >= 6:
        W("y", SC)
        a2av = a2a_out.rearrange("r (jj p t) -> p r jj t", jj=2, p=128)
        wov = wo_in.rearrange("(kv p) m -> p kv m", p=128)
        for half in range(2):
            D("y", aow[half][:, :].rearrange("p (r jj tt) -> p r jj tt", r=8, jj=2),
              a2av[:, :, :, half * 512:half * 512 + 512], SY)
        for mg in range(4):
            bi = mg % 2
            if mg >= 2:
                W("g", ST_, M(ST_))
            D("g", wo_buf[bi][:, :].rearrange("p (kv mm) -> p kv mm", kv=16),
              wov[:, :, mg * 512:(mg + 1) * 512], f"go{bi}")
            W("t", f"go{bi}")
            W("t", SY)
            for mt in range(4):
                dt = mg * 4 + mt
                for half in range(2):
                    pw = pb[(dt * 2 + half) % 2]
                    if dt * 2 + half >= 2:
                        W("t", SV, M(SV) - 1)
                    for kv in range(KD):
                        E("t", (lambda b_, k_, m_, h_, p_: lambda t: t.matmul(
                            p_[:, 0:512], wo_buf[b_][:, k_ * 512 + m_ * 128:k_ * 512 + m_ * 128 + 128],
                            aow[h_][:, k_ * 512:k_ * 512 + 512],
                            start=(k_ == 0), stop=(k_ == KD - 1)))(bi, kv, mt, half, pw),
                          inc=(ST_ if kv == KD - 1 else None))
                    W("v", ST_)
                    E("v", (lambda p_, c_: lambda v: v.scalar_tensor_tensor(
                        resid[:, c_:c_ + 512], p_[:, 0:512], 1.0, resid[:, c_:c_ + 512],
                        op0=ALU.mult, op1=ALU.add))(pw, dt * 1024 + half * 512), inc=SV)
        B.barrier()
        if DBG:
            D("y", dbg["dbg_resid2"], resid[:, :], SY)
            B.barrier()

    if PHASE_LIMIT >= 7:
        B.phase = "P7_ln3"
        ln_phase()
        B.phase = "P7_ffn2"
        ffn_phase(first=False)

    B.phase = "P8_out"
    D("y", yout.rearrange("(kd p) t -> p kd t", p=128), resid[:, :], SY)
    B.barrier()
    W("y", SY)

    with ctx:
        with nc.Block() as block:
            with contextlib.ExitStack() as semctx:
                for sname in B.cnt:
                    B.sems[sname] = semctx.enter_context(nc.semaphore(sname))
                with allow_ici_all_to_all():
                    B.replay(block, nc)
    return nc


def make_inputs(full, core):
    c = core
    h0, h1 = 2 * c, 2 * c + 1
    x = np.asarray(full["L0_attn_x"])[0]
    w1 = np.asarray(full["L0_ffn_w1"]); w2 = np.asarray(full["L0_ffn_w2"]); w3 = np.asarray(full["L0_ffn_w3"])
    wqd = np.asarray(full["L0_attn_wq_down"]); qnw_ = np.asarray(full["L0_attn_q_norm_w"])
    wqu = np.asarray(full["L0_attn_wq_up"]); wqr = np.asarray(full["L0_attn_wq_rope"])
    wkvd = np.asarray(full["L0_attn_wkv_down"]); kvnw_ = np.asarray(full["L0_attn_kv_norm_w"])
    wkvu = np.asarray(full["L0_attn_wkv_up"]); wkr = np.asarray(full["L0_attn_wk_rope"])
    wo = np.asarray(full["L0_attn_wo"])
    bf = ml_dtypes.bfloat16
    C = np.ascontiguousarray

    pos = np.arange(S_ALL, dtype=np.float64)
    inv = 1.0 / 10000.0 ** (np.arange(0, 64, 2, dtype=np.float64) / 64.0)
    ang = pos[:, None] * inv[None]
    cosv = np.cos(ang).astype(np.float32); sinv = np.sin(ang).astype(np.float32)
    c64 = np.repeat(cosv, 2, axis=1).T
    s64 = np.repeat(sinv, 2, axis=1).T
    cos128 = np.concatenate([c64, c64], 0)
    sin128 = np.concatenate([s64, s64], 0)
    toks = slice(TOK * c, TOK * (c + 1))

    pm = np.zeros((64, 64), np.float32)
    for i in range(32):
        pm[2 * i, 2 * i + 1] = -1.0
        pm[2 * i + 1, 2 * i] = 1.0
    pmfull = np.zeros((128, 128), np.float32)
    pmfull[:64, :64] = pm
    pmfull[64:, 64:] = pm
    pmT_ = pmfull.T

    maskp_ = np.zeros((128, 4, 512), np.float32)
    masktp_ = np.zeros((128, 4, 512), np.float32)
    for phs in range(4):
        qloc = phs * 128 + np.arange(128)
        kloc = np.arange(512)
        maskp_[:, phs, :] = np.where(kloc[None, :] <= qloc[:, None], 0.0, -1e9)
    masktp_ = np.zeros((128, 4, 512), np.float32)
    for ks in range(4):
        krow = ks * 128 + np.arange(128)
        qq = np.arange(512)
        masktp_[:, ks, :] = np.where(krow[:, None] <= qq[None, :], 0.0, -1e9)

    wq_head_ = np.concatenate([wqu[:, 128 * h0:128 * h0 + 128], wqu[:, 128 * h1:128 * h1 + 128],
                               wqr[:, 64 * h0:64 * h0 + 64], wqr[:, 64 * h1:64 * h1 + 64]], axis=1)
    return dict(
        xT=C(x[toks].T).astype(np.float32),
        w1_s=C(w1[:, 256 * c:256 * (c + 1)].T),
        w2_s=C(w2[:, 256 * c:256 * (c + 1)].T),
        w3_s=C(w3[256 * c:256 * (c + 1), :].T),
        wq_down=wqd, wkv_down=wkvd, wk_rope=wkr, wo=wo,
        wq_head=C(wq_head_),
        wkv_k=C(np.concatenate([wkvu[:, 256 * h0:256 * h0 + 128],
                                wkvu[:, 256 * h1:256 * h1 + 128]], 1)),
        wkv_v=C(np.concatenate([wkvu[:, 256 * h0 + 128:256 * h0 + 256],
                                wkvu[:, 256 * h1 + 128:256 * h1 + 256]], 1)),
        qnw_t=C(qnw_.reshape(QKD, 128).T),
        kvnw_t=C(kvnw_.reshape(KVKD, 128).T),
        onesr=np.ones((1, 128), np.float32),
        idf=np.eye(128, dtype=np.float32),
        idb=np.eye(128, dtype=np.float32).astype(bf),
        pmT=pmT_.astype(bf),
        maskp=C(maskp_.reshape(128, 4 * 512)).astype(bf),
        masktp=C(masktp_.reshape(128, 4 * 512)).astype(bf),
        ropecq=C(cos128 * SCALE).astype(bf),
        ropesq=C(sin128 * SCALE).astype(bf),
        ropeck=C(cos128[:, toks]).astype(bf),
        ropesk=C(sin128[:, toks]).astype(bf),
    )


def kernel(**inputs):
    nc = build_program()
    in_maps = [make_inputs(inputs, c) for c in range(NCORE)]
    res = run_bass_kernel_spmd(nc, in_maps, core_ids=list(range(NCORE)))
    out = np.concatenate([res.results[c]["y"].T for c in range(NCORE)], axis=0)
    return out.reshape(1, S_ALL, DM).astype(np.float32)
